# version 24
# baseline (speedup 1.0000x reference)
import contextlib
import numpy as np
import ml_dtypes
import concourse.bass as bass
import concourse.mybir as mybir

F32 = mybir.dt.float32
BF16 = mybir.dt.bfloat16
AF = mybir.ActivationFunctionType
ALU = mybir.AluOpType
AX = mybir.AxisListType
bfnp = ml_dtypes.bfloat16

ENGINES = ("pe", "act", "dve", "pool", "sp")


class Tok:
    __slots__ = ("eng", "idx", "sem", "val")

    def __init__(self, eng, idx):
        self.eng, self.idx, self.sem, self.val = eng, idx, None, None


class Op:
    __slots__ = ("fn", "deps", "tok", "dma_group", "needs_inc", "tag", "gseq")

    def __init__(self, fn, deps, tok, dma_group, tag, gseq):
        self.fn, self.deps, self.tok = fn, deps, tok
        self.dma_group, self.needs_inc, self.tag = dma_group, False, tag
        self.gseq = gseq


class Prog:
    def __init__(self, nc):
        self.nc = nc
        self.ops = {e: [] for e in ENGINES}
        self.n_emitted = 0
        self.last_dma = {}
        self.group_mode = {}

    def emit(self, eng, fn, deps=(), dma_group=None, tag=""):
        dl = []
        seen = {}
        for d in deps:
            if d is None:
                continue
            m = False
            if isinstance(d, tuple):
                m = True
                d = d[1]
            k = (d.eng, d.idx)
            if k in seen:
                if m and not seen[k][1]:
                    seen[k][0][:] = [d, True]
                continue
            ent = [d, m]
            seen[k] = (ent, m)
            dl.append(ent)
        tok = Tok(eng, len(self.ops[eng]))
        self.ops[eng].append(Op(fn, dl, tok, dma_group, tag, self.n_emitted))
        self.n_emitted += 1
        return tok

    def dma(self, eng, out_ap, in_ap, deps=(), group=None, serial=True, tag="", **kw):
        assert group is not None
        mode = (eng, "serial" if serial else "batch")
        assert self.group_mode.setdefault(group, mode) == mode, (group, mode)

        def fn(e):
            return e.dma_start(out_ap, in_ap, **kw)

        deps = list(deps)
        if serial and group in self.last_dma:
            deps.append(self.last_dma[group])
        tok = self.emit(eng, fn, deps=deps, dma_group=group, tag=tag)
        self.last_dma[group] = tok
        return tok

    def finalize_and_emit(self, ctx):
        nc = self.nc

        def elided(e, d):
            dop = self.ops[d.eng][d.idx]
            return e == "pe" and d.eng == "pe" and dop.dma_group is None

        def margin_tok(d, consumer):
            lst = self.ops[d.eng]
            j = d.idx + 1
            if j < len(lst) and lst[j].gseq < consumer.gseq:
                return lst[j]
            return None

        for e in ENGINES:
            for op in self.ops[e]:
                if op.dma_group is not None:
                    op.needs_inc = True
                for d, m in op.deps:
                    if elided(e, d):
                        continue
                    self.ops[d.eng][d.idx].needs_inc = True
                    if m:
                        mo = margin_tok(d, op)
                        if mo is not None and mo.dma_group is None:
                            mo.needs_inc = True
        sems = {}

        def get_sem(name):
            if name not in sems:
                sems[name] = ctx.enter_context(nc.semaphore(name=name))
            return sems[name]

        counts = {}
        for e in ENGINES:
            for op in self.ops[e]:
                if not op.needs_inc:
                    continue
                if op.dma_group is not None:
                    sname, amt = "d_" + op.dma_group, 16
                else:
                    sname, amt = "c_" + e, 1
                counts[sname] = counts.get(sname, 0) + amt
                op.tok.sem, op.tok.val = sname, counts[sname]
        stats = {"waits": 0, "incs": 0, "sems": len(counts), "ops": self.n_emitted}

        def make_runner(e):
            oplist = self.ops[e]

            def run(engine_obj):
                hiwater = {}
                for op in oplist:
                    waits = {}
                    for d, m in op.deps:
                        if elided(e, d):
                            continue
                        s, v = d.sem, d.val
                        if m:
                            mo = margin_tok(d, op)
                            if mo is not None and mo.dma_group is None:
                                s, v = mo.tok.sem, mo.tok.val
                        assert s is not None, (e, op.tag, d.eng, d.idx)
                        if hiwater.get(s, 0) >= v:
                            continue
                        if waits.get(s, 0) < v:
                            waits[s] = v
                    for s, v in waits.items():
                        engine_obj.wait_ge(get_sem(s), v)
                        hiwater[s] = v
                        stats["waits"] += 1
                    ins = op.fn(engine_obj)
                    if op.needs_inc:
                        amt = 16 if op.dma_group is not None else 1
                        ins.then_inc(get_sem(op.tok.sem), amt)
                        stats["incs"] += 1

            return run

        with nc.Block() as block:
            block.tensor(make_runner("pe"))
            block.scalar(make_runner("act"))
            block.vector(make_runner("dve"))
            block.gpsimd(make_runner("pool"))
            block.sync(make_runner("sp"))
        return stats


class Buf:

    def __init__(self, nslots=1, margin=False):
        self.n = nslots
        self.margin = margin
        self.w = [[] for _ in range(nslots)]
        self.r = [[] for _ in range(nslots)]

    def _sl(self, s):
        if s is None:
            return range(self.n)
        if isinstance(s, int):
            return (s,)
        return s

    def deps_write(self, s=None):
        out = []
        for i in self._sl(s):
            out += self.w[i]
            if self.margin:
                out += [("m", t) for t in self.r[i]]
            else:
                out += self.r[i]
        return out

    def deps_read(self, s=None):
        out = []
        for i in self._sl(s):
            out += self.w[i]
        return out

    def wrote(self, tok, s=None):
        for i in self._sl(s):
            self.w[i] = [tok]
            self.r[i] = []

    def accum(self, tok, s=None):
        for i in self._sl(s):
            self.w[i].append(tok)

    def read(self, tok, s=None):
        for i in self._sl(s):
            self.r[i].append(tok)


DM = 1024
S = 2048
BPC = 2
H = 4096
CC = 8
HC = 32
HEADS = 8
DH = 128
NHALF = 2
HPH = 4
SCALE_ATTN = DH ** -0.5
CUSH = 30.0
NT = 1024
LN_EPS = 1e-5
PSUM_MARGIN = False


def host_constants():
    c = {}
    c["ones_col_bf"] = np.ones((128, 1), bfnp)
    c["ones_col_f32"] = np.ones((128, 1), np.float32)
    c["ones_row_f32"] = np.ones((1, 128), np.float32)
    c["negones_row_f32"] = -np.ones((1, 128), np.float32)
    c["ident_bf"] = np.eye(128, dtype=bfnp)
    j = np.arange(896)[None, :] - 384.0
    k = np.arange(128)[:, None]
    c["mask_master_bf"] = np.where(j >= k, 0.0, -1e30).astype(bfnp)

    def sel(rows):
        m = np.zeros((16, 4), np.float32)
        for i, r in enumerate(rows):
            m[r, i] = 1.0
        return m.astype(bfnp)

    c["o1sel_bf"] = sel([1, 3, 5, 7])
    c["o2sel_bf"] = sel([9, 11, 13, 15])
    c["e1sel_bf"] = sel([0, 2, 4, 6])
    c["e2sel_bf"] = sel([8, 10, 12, 14])
    return c


def build(n_sub=3):
    nc = bass.Bass("TRN2", target_bir_lowering=False, debug=False)
    C = host_constants()

    xT = nc.dram_tensor("xT", [BPC, CC, 128, S], F32, kind="ExternalInput").ap()
    outT = nc.dram_tensor("outT", [BPC, CC, 128, S], F32, kind="ExternalOutput").ap()
    w1T = nc.dram_tensor("w1T", [DM, H], F32, kind="ExternalInput").ap()
    w2T = nc.dram_tensor("w2T", [H, DM], F32, kind="ExternalInput").ap()
    wqkvT = nc.dram_tensor("wqkvT", [DM, 3 * DM], F32, kind="ExternalInput").ap()
    woutT = nc.dram_tensor("woutT", [DM, DM], F32, kind="ExternalInput").ap()
    b1h = nc.dram_tensor("b1h", [128, HC], F32, kind="ExternalInput").ap()
    b2h = nc.dram_tensor("b2h", [128, CC], F32, kind="ExternalInput").ap()
    gamh = nc.dram_tensor("gamh", [128, CC], F32, kind="ExternalInput").ap()
    scaleh = nc.dram_tensor("scaleh", [1, 1], F32, kind="ExternalInput").ap()

    consts = {k: nc.inline_tensor(v, name=k).ap() for k, v in C.items()}
    e2scr = nc.dram_tensor("e2scr", [4, DM], BF16, kind="Internal").ap()
    w1s = nc.dram_tensor("w1s", [DM, H], BF16, kind="Internal").ap()
    w2s = nc.dram_tensor("w2s", [H, DM], BF16, kind="Internal").ap()
    wqs = nc.dram_tensor("wqs", [DM, 3 * DM], BF16, kind="Internal").ap()
    wos = nc.dram_tensor("wos", [DM, DM], BF16, kind="Internal").ap()
    uscr = nc.dram_tensor("uscr", [4, 4 * 512], BF16, kind="Internal").ap()
    browL = nc.dram_tensor("browL", [1, 1024], F32, kind="Internal").ap()
    browF = nc.dram_tensor("browF", [1, 512], F32, kind="Internal").ap()
    browN = nc.dram_tensor("browN", [1, 8], F32, kind="Internal").ap()

    p = Prog(nc)
    bufs = {}
    ctx = contextlib.ExitStack()
    with ctx:
        def sb(name, shape, dtype, nslots=1):
            t = ctx.enter_context(nc.sbuf_tensor(name, shape, dtype))
            bufs[name] = Buf(nslots)
            return t

        R = sb("R", [128, CC * S], F32, nslots=32)
        arena = sb("arena", [128, HC * NT], BF16, nslots=32)
        xn = sb("xn", [128, CC * S], BF16, nslots=32)
        wb0 = sb("wb0", [128, 1024], BF16)
        wb1 = sb("wb1", [128, 1024], BF16)
        wb2 = sb("wb2", [128, 1024], BF16)
        f32t = sb("f32t", [128, 512], F32)
        ln_t = f32t
        tmp_o = f32t
        recb = f32t
        sqt = sb("sqt", [128, 1024], BF16, nslots=2)
        qsq = sqt
        srow_a = sb("srow_a", [1, 512], F32)
        srow_b = sb("srow_b", [1, 512], F32)
        srow_c = sb("srow_c", [1, 512], F32)
        e1z = sb("e1z", [4, H], BF16)
        t1 = sb("t1", [4, NT], BF16)
        lt = sb("lt", [16, DM], BF16)

        e2d = sb("e2d", [4, CC * 128], BF16)
        u4 = sb("u4", [4, 4 * 512], BF16)
        e24 = u4[:, 0:DM]
        e2st = sb("e2st", [128, 32], BF16)
        pb = sb("pb", [128, 512 * 4], BF16, nslots=4)
        recr = srow_a
        rep_a = sb("rep_a", [128, 512], F32)
        rep_b = sb("rep_b", [128, 512], F32)
        nrm = sb("nrm", [1, 64], F32)
        negm = sb("negm", [128, HPH], F32, nslots=HPH)
        b1c = sb("b1c", [128, HC], F32)
        b2c = sb("b2c", [128, CC], F32)
        gamc = sb("gamc", [128, CC], F32)
        scol = sb("scol", [16, 1], F32)
        onesc = sb("onesc", [128, 1], BF16)
        onesr = sb("onesr", [1, 128], F32)
        negr = sb("negr", [1, 128], F32)
        onesf = sb("onesf", [128, 1], F32)
        ident = sb("ident", [128, 128], BF16)
        maskm = sb("maskm", [128, 896], BF16)
        o1s = sb("o1s", [16, 4], BF16)
        o2s = sb("o2s", [16, 4], BF16)
        e1s = sb("e1s", [16, 4], BF16)
        e2s = sb("e2s", [16, 4], BF16)
        o1sc = sb("o1sc", [16, 4], BF16)
        o2sc = sb("o2sc", [16, 4], BF16)
        o1c = sb("o1c", [128, CC * 4], BF16)
        o2c = sb("o2c", [128, CC * 4], BF16)
        sbuf_left = nc.sbuf_bytes_remaining
        ps = []
        for i in range(8):
            ps.append(ctx.enter_context(nc.psum_tensor(f"ps{i}", [128, 512], F32)))
            bufs[f"ps{i}"] = Buf(margin=PSUM_MARGIN)

        def B(name):
            return bufs[name]

        def rsl(cc, t0_, w=512):
            return R[:, cc * S + t0_: cc * S + t0_ + w]

        def rslot(cc, t0_):
            return cc * 4 + t0_ // 512

        def xsl(cc, t0_, w=512):
            return xn[:, cc * S + t0_: cc * S + t0_ + w]

        def h_act(hc, nh):
            return arena[:, hc * NT + nh * 512: hc * NT + (nh + 1) * 512]

        def h_slot(hc):
            return hc

        KT0, V0, QT0, OA0 = 0, HPH * S, 2 * HPH * S, 3 * HPH * S

        def aslot(col0, width):
            return list(range(col0 // NT, (col0 + width + NT - 1) // NT))

        ld = []
        for dst, src, tag in ((b1c, b1h, "b1"), (b2c, b2h, "b2"), (gamc, gamh, "gam"),
                              (onesc, consts["ones_col_bf"], "oc"),
                              (onesr, consts["ones_row_f32"], "or"),
                              (negr, consts["negones_row_f32"], "nr"),
                              (onesf, consts["ones_col_f32"], "of"),
                              (ident, consts["ident_bf"], "id"),
                              (maskm, consts["mask_master_bf"], "mask"),
                              (o1s, consts["o1sel_bf"], "o1s"),
                              (o2s, consts["o2sel_bf"], "o2s"),
                              (e1s, consts["e1sel_bf"], "e1s"),
                              (e2s, consts["e2sel_bf"], "e2s")):
            t = p.dma("sp", dst[:, :], src, group="ini", serial=False, tag=tag)
            ld.append(t)
        ini_tok = p.emit("dve", lambda e: e.engine_nop(), deps=ld, tag="ini_fence")
        for name in ("b1c", "b2c", "gamc", "onesc", "onesf", "onesr", "negr", "ident",
                     "maskm", "o1s", "o2s", "e1s", "e2s", "srow_a"):
            B(name).wrote(ini_tok)
        t_sc_ev = p.dma("sp", scol[:, :], scaleh[0:1, 0:1].broadcast_to([16, 1]),
                        deps=[ini_tok], group="scb", tag="scbc")
        B("scol").wrote(t_sc_ev)
        for src_s, dst_s, nm in ((o1s, o1sc, "o1sc"), (o2s, o2sc, "o2sc")):
            t = p.emit("dve", lambda e, src_s=src_s, dst_s=dst_s: e.tensor_scalar_mul(
                dst_s[:, :], src_s[:, :], scol[:, 0:1]),
                deps=[t_sc_ev, ini_tok], tag=nm)
            B(nm).wrote(t)
            B("scol").read(t)
        t_ms = p.emit("dve", lambda e: e.memset(e1z[:, :], 0.0), deps=[], tag="e1z_memset")
        B("e1z").wrote(t_ms)

        prep_done = {}

        def do_prep():
            blocks = []
            for b in range(4):
                src = w1T[:, b * 1024:(b + 1) * 1024].rearrange("(g p) h -> p g h", p=128)
                dst = w1s[:, b * 1024:(b + 1) * 1024].rearrange("(g p) h -> p g h", p=128)
                blocks.append(("w1", b, src, dst, 8192))
            for b in range(4):
                src = w2T[b * 1024:(b + 1) * 1024, :].rearrange("(g p) o -> p g o", p=128)
                dst = w2s[b * 1024:(b + 1) * 1024, :].rearrange("(g p) o -> p g o", p=128)
                blocks.append(("w2", b, src, dst, 8192))
            for b in range(4):
                src = wqkvT[b * 256:(b + 1) * 256, :].rearrange("(g p) f -> p g f", p=128)
                dst = wqs[b * 256:(b + 1) * 256, :].rearrange("(g p) f -> p g f", p=128)
                blocks.append(("wq", b, src, dst, 6144))
            for b in range(2):
                src = woutT[b * 512:(b + 1) * 512, :].rearrange("(g p) c -> p g c", p=128)
                dst = wos[b * 512:(b + 1) * 512, :].rearrange("(g p) c -> p g c", p=128)
                blocks.append(("wo", b, src, dst, 4096))
            for kind in ("w1", "w2", "wq", "wo"):
                prep_done[kind] = {}
            return blocks

        def emit_prep(blocks):
            for bi, (kind, b, src, dst, width) in enumerate(blocks):
                par = bi % 2
                slot0 = par * 8
                stg = arena[:, slot0 * NT: slot0 * NT + width]
                sl = list(range(slot0, slot0 + 8))
                stg3 = stg.rearrange("p (g x) -> p g x", x=src.shape[2])
                t_l = p.dma("pool", stg3, src, deps=B("arena").deps_write(sl),
                            group=f"prep_l{par}", tag=f"pl_{kind}{b}")
                B("arena").wrote(t_l, sl)
                t_s = p.dma("sp", dst, stg3, deps=[t_l], group=f"prep_s{par}",
                            tag=f"ps_{kind}{b}")
                B("arena").read(t_s, sl)
                prep_done[kind][b] = t_s
            prep_done["wk"] = prep_done["wv"] = prep_done["wq"]

        def wslab(kind, arg, wt, wbname, part=0):
            if kind == "w1":
                gate = prep_done["w1"][arg // 8]
                src = w1s[:, arg * 128:(arg + 1) * 128].rearrange("(cc p) h -> p cc h", p=128)
                dst = wt[:, :].rearrange("p (cc h) -> p cc h", cc=CC)
            elif kind == "w2":
                gate = prep_done["w2"][arg // 8]
                src = w2s[arg * 128:(arg + 1) * 128, :].rearrange("(hh p) o -> p hh o", p=128)
                dst = wt[:, :].rearrange("p (hh o) -> p hh o", hh=1)
            elif kind in ("wq", "wk"):
                gate = prep_done["wq"][3]
                f0 = {"wq": 0, "wk": 1}[kind] * DM + arg * 512 + part * 128
                src = wqs[:, f0:f0 + 128].rearrange("(cc p) f -> p cc f", p=128)
                dst = wt[:, :].rearrange("p (cc f) -> p cc f", cc=CC)
            else:
                gate = prep_done["wo"][arg]
                r0 = arg * 512 + part * 128
                src = wos[r0:r0 + 128, :].rearrange("(fc p) c -> p fc c", p=128)
                dst = wt[:, :].rearrange("p (fc c) -> p fc c", fc=1)
            deps = B(wbname).deps_write() + [gate]
            grp = {"wb0": "wA", "wb1": "wB", "wb2": "wC"}[wbname]
            t = p.dma("sp", dst, src, deps=deps, group=grp, tag=f"{kind}{arg}.{part}")
            B(wbname).wrote(t)
            return t

        wrot = [0]
        _wbs = ((wb0, "wb0"), (wb1, "wb1"), (wb2, "wb2"))

        def next_w():
            i = wrot[0] % 3
            wrot[0] += 1
            return _wbs[i]

        def layer_norm(t0, ntok, gamma_fused, bcast_pe=False, sq_eng="act"):
            for n0 in range(t0, t0 + ntok, 512):
                ln_block(n0, gamma_fused, bcast_pe, sq_eng)

        def ln_block(n0, gamma_fused, bcast_pe=False, sq_eng="act"):
            if True:
                ta = tb = None
                for cc in range(CC):
                    t_sa = p.emit("pe", lambda e, cc=cc, n0=n0: e.matmul(
                        ps[6][0:1, :], onesf[:, :], rsl(cc, n0), start=(cc == 0), stop=(cc == CC - 1)),
                        deps=B("R").deps_read(rslot(cc, n0)) + B("onesf").deps_read() +
                             (B("ps6").deps_write() if cc == 0 else []),
                        tag="sumx")
                    B("R").read(t_sa, rslot(cc, n0))
                    B("onesf").read(t_sa)
                    ta = t_sa
                for cc in range(CC):
                    pr = cc % 2
                    if sq_eng == "act":
                        sqf = lambda e, cc=cc, n0=n0, pr=pr: e.activation(
                            sqt[:, pr * 512:(pr + 1) * 512], rsl(cc, n0), AF.Square)
                    else:
                        sqf = lambda e, cc=cc, n0=n0, pr=pr: e.tensor_mul(
                            sqt[:, pr * 512:(pr + 1) * 512], rsl(cc, n0), rsl(cc, n0))
                    t_sq = p.emit(sq_eng, sqf,
                        deps=B("R").deps_read(rslot(cc, n0)) + B("sqt").deps_write(pr), tag="sq")
                    B("sqt").wrote(t_sq, pr)
                    B("R").read(t_sq, rslot(cc, n0))
                    t_sb2 = p.emit("pe", lambda e, cc=cc, pr=pr: e.matmul(
                        ps[7][0:1, :], onesc[:, :], sqt[:, pr * 512:(pr + 1) * 512],
                        start=(cc == 0), stop=(cc == CC - 1)),
                        deps=[t_sq] + B("onesc").deps_read() + (B("ps7").deps_write() if cc == 0 else []),
                        tag="sumx2")
                    B("sqt").read(t_sb2, pr)
                    B("onesc").read(t_sb2)
                    tb = t_sb2
                B("ps6").wrote(ta)
                B("ps7").wrote(tb)
                t_ea = p.emit("act", lambda e: e.mul(srow_a[:, :], ps[6][0:1, :], 1.0 / DM),
                              deps=[ta] + B("srow_a").deps_write(), tag="mu")
                B("srow_a").wrote(t_ea)
                B("ps6").read(t_ea)
                t_eb = p.emit("act", lambda e: e.mul(srow_b[:, :], ps[7][0:1, :], 1.0 / DM),
                              deps=[tb] + B("srow_b").deps_write(), tag="ex2")
                B("srow_b").wrote(t_eb)
                B("ps7").read(t_eb)
                t_c = p.emit("dve", lambda e: e.tensor_mul(srow_c[:, :], srow_a[:, :], srow_a[:, :]),
                             deps=[t_ea] + B("srow_c").deps_write(), tag="mu2")
                B("srow_c").wrote(t_c)
                B("srow_a").read(t_c)
                t_v = p.emit("dve", lambda e: e.tensor_sub(srow_b[:, :], srow_b[:, :], srow_c[:, :]),
                             deps=[t_eb, t_c], tag="var")
                B("srow_b").wrote(t_v)
                B("srow_c").read(t_v)
                eps = 1e-30 if gamma_fused else LN_EPS
                t_ve = p.emit("dve", lambda e, eps=eps: e.tensor_scalar_add(
                    srow_b[:, :], srow_b[:, :], float(eps)), deps=[t_v], tag="veps")
                B("srow_b").wrote(t_ve)
                t_ln = p.emit("act", lambda e: e.activation(
                    srow_b[:, :], srow_b[:, :], AF.Ln), deps=[t_ve], tag="lnv")
                t_rsd = p.emit("act", lambda e: e.activation(
                    srow_b[:, :], srow_b[:, :], AF.Exp, scale=-0.5), deps=[t_ln], tag="rstd")
                B("srow_b").wrote(t_rsd)
                t_mr = p.emit("dve", lambda e: e.tensor_mul(srow_c[:, :], srow_a[:, :], srow_b[:, :]),
                              deps=[t_rsd, t_ea] + B("srow_c").deps_write(), tag="murstd")
                B("srow_c").wrote(t_mr)
                B("srow_a").read(t_mr)
                B("srow_b").read(t_mr)
                if bcast_pe:
                    t_b1m = p.emit("pe", lambda e: e.matmul(ps[6][:, :], onesr[:, :], srow_b[:, :],
                                                            start=True, stop=True),
                                   deps=[t_rsd] + B("ps6").deps_write() + B("onesr").deps_read(),
                                   tag="bcp1")
                    B("ps6").wrote(t_b1m)
                    B("srow_b").read(t_b1m)
                    B("onesr").read(t_b1m)
                    t_bc1 = p.emit("act", lambda e: e.copy(rep_a[:, :], ps[6][:, :]),
                                   deps=[t_b1m] + B("rep_a").deps_write(), tag="bcp1e")
                    B("ps6").read(t_bc1)
                    B("rep_a").wrote(t_bc1)
                    t_mr2 = B("srow_c").deps_read()
                    t_b2m = p.emit("pe", lambda e: e.matmul(ps[7][:, :], negr[:, :], srow_c[:, :],
                                                            start=True, stop=True),
                                   deps=[t_mr] + B("ps7").deps_write() + B("negr").deps_read(),
                                   tag="bcp2")
                    B("ps7").wrote(t_b2m)
                    B("srow_c").read(t_b2m)
                    B("negr").read(t_b2m)
                    t_bc2 = p.emit("act", lambda e: e.copy(rep_b[:, :], ps[7][:, :]),
                                   deps=[t_b2m] + B("rep_b").deps_write(), tag="bcp2e")
                    B("ps7").read(t_bc2)
                    B("rep_b").wrote(t_bc2)
                else:
                    t_neg = p.emit("dve", lambda e: e.tensor_scalar_mul(srow_c[:, :], srow_c[:, :], -1.0),
                                   deps=[t_mr], tag="negmur")
                    B("srow_c").wrote(t_neg)
                    t_o1 = p.dma("act", browL[0:1, 0:512], srow_b[:, :], deps=[t_rsd],
                                 group="browL", tag="brL1o")
                    B("srow_b").read(t_o1)
                    t_o2 = p.dma("act", browL[0:1, 512:1024], srow_c[:, :], deps=[t_neg],
                                 group="browL", tag="brL2o")
                    B("srow_c").read(t_o2)
                    t_bc1 = p.dma("act", rep_a[:, :], browL[0:1, 0:512].broadcast_to([128, 512]),
                                  deps=[t_o1] + B("rep_a").deps_write(), group="browL", tag="brL1i")
                    B("rep_a").wrote(t_bc1)
                    t_bc2 = p.dma("act", rep_b[:, :], browL[0:1, 512:1024].broadcast_to([128, 512]),
                                  deps=[t_o2] + B("rep_b").deps_write(), group="browL", tag="brL2i")
                    B("rep_b").wrote(t_bc2)
                ap_eng = "pool" if (sq_eng == "pool" and not gamma_fused) else "dve"
                for cc in range(CC):
                    xslot = rslot(cc, n0)
                    if not gamma_fused:
                        t_p1 = p.emit(ap_eng, lambda e, cc=cc, n0=n0: e.tensor_mul(
                            ln_t[:, :], rsl(cc, n0), rep_a[:, :]),
                            deps=[t_bc1] + B("f32t").deps_write() + B("R").deps_read(rslot(cc, n0)),
                            tag="lnap1")
                        t_p2f = lambda e, cc=cc, n0=n0: e.tensor_add(
                            xsl(cc, n0), rep_b[:, :], ln_t[:, :])
                    elif sq_eng == "pool":
                        t_p0 = p.emit("pool", lambda e, cc=cc, n0=n0: e.tensor_mul(
                            ln_t[:, :], rsl(cc, n0), rep_a[:, :]),
                            deps=[t_bc1] + B("f32t").deps_write() + B("R").deps_read(rslot(cc, n0)),
                            tag="lnap0g")
                        B("rep_a").read(t_p0)
                        B("R").read(t_p0, rslot(cc, n0))
                        B("f32t").wrote(t_p0)
                        t_p1 = p.emit("pool", lambda e: e.tensor_add(
                            ln_t[:, :], ln_t[:, :], rep_b[:, :]),
                            deps=[t_p0, t_bc2], tag="lnap1g2")
                        B("rep_b").read(t_p1)
                        B("f32t").wrote(t_p1)
                        t_p2f = lambda e, cc=cc, n0=n0: e.tensor_scalar_mul(
                            xsl(cc, n0), ln_t[:, :], gamc[:, cc:cc + 1])
                    else:
                        t_p1 = p.emit(ap_eng, lambda e, cc=cc, n0=n0: e.scalar_tensor_tensor(
                            ln_t[:, :], rsl(cc, n0), gamc[:, cc:cc + 1], rep_a[:, :],
                            ALU.mult, ALU.mult),
                            deps=[t_bc1] + B("f32t").deps_write() + B("R").deps_read(rslot(cc, n0)) +
                                 B("gamc").deps_read(), tag="lnap1g")
                        t_p2f = lambda e, cc=cc, n0=n0: e.scalar_tensor_tensor(
                            xsl(cc, n0), rep_b[:, :], gamc[:, cc:cc + 1], ln_t[:, :],
                            ALU.mult, ALU.add)
                    B("rep_a").read(t_p1)
                    B("R").read(t_p1, rslot(cc, n0))
                    B("f32t").wrote(t_p1)
                    t_p2 = p.emit(ap_eng, t_p2f,
                                  deps=[t_bc2, t_p1] + B("xn").deps_write(xslot), tag="lnap2")
                    B("rep_b").read(t_p2)
                    B("f32t").read(t_p2)
                    B("xn").wrote(t_p2, xslot)

        def lora_prep(part=0):
            xdeps = B("xn").deps_read([rslot(cc, 0) for cc in range(CC)])
            p6b = ps[6][:, :].bitcast(BF16)
            lt_toks = []
            for cc in (() if part == 2 else range(CC)):
                t_tr = p.emit("pe", lambda e, cc=cc: e.transpose(
                    p6b[0:16, 0:128], xn[:, cc * S + 1: cc * S + 17], ident[:, :]),
                    deps=xdeps + B("ps6").deps_write() + B("ident").deps_read(), tag="lt_tr")
                B("ps6").wrote(t_tr)
                B("xn").read(t_tr, rslot(cc, 0))
                B("ident").read(t_tr)
                t_ev = p.emit("act", lambda e, cc=cc: e.copy(
                    lt[:, cc * 128:(cc + 1) * 128], p6b[0:16, 0:128]),
                    deps=[t_tr] + (B("lt").deps_write() if cc == 0 else []), tag="lt_ev")
                B("ps6").read(t_ev)
                if cc == 0:
                    B("lt").wrote(t_ev)
                else:
                    B("lt").accum(t_ev)
                lt_toks.append(t_ev)
            if part == 2:
                lt_toks = list(B("lt").deps_read())
            sel_list = ((o1sc, "o1sc", o1c, "o1c"), (o2sc, "o2sc", o2c, "o2c"))
            if part == 1:
                sel_list = sel_list[:1]
            elif part == 2:
                sel_list = sel_list[1:]
            for sel_t, sel_n, dst_t, dst_n in sel_list:
                mms = []
                for cc in range(CC):
                    t_mm = p.emit("pe", lambda e, cc=cc, sel_t=sel_t: e.matmul(
                        ps[7][0:128, cc * 4:(cc + 1) * 4], lt[:, cc * 128:(cc + 1) * 128],
                        sel_t[:, :], start=(cc == 0), stop=(cc == CC - 1)),
                        deps=lt_toks + B(sel_n).deps_read() +
                             (B("ps7").deps_write() if cc == 0 else []), tag=dst_n + "mm")
                    B("lt").read(t_mm)
                    B(sel_n).read(t_mm)
                    mms.append(t_mm)
                B("ps7").wrote(mms[-1])
                t_ev = p.emit("act", lambda e, dst_t=dst_t: e.copy(
                    dst_t[:, :], ps[7][0:128, 0:32]),
                    deps=[mms[-1]] + B(dst_n).deps_write(), tag=dst_n + "ev")
                B("ps7").read(t_ev)
                B(dst_n).wrote(t_ev)
            for half in (() if part == 2 else range(2)):
                t_mm = p.emit("pe", lambda e, half=half: e.matmul(
                    ps[7][0:4, :], e1s[:, :], lt[:, half * 512:(half + 1) * 512],
                    start=True, stop=True),
                    deps=lt_toks + B("ps7").deps_write() + B("e1s").deps_read(), tag="e1mm")
                B("ps7").wrote(t_mm)
                B("lt").read(t_mm)
                B("e1s").read(t_mm)
                t_ev = p.emit("act", lambda e: e.copy(t1[0:4, 0:512], ps[7][0:4, 0:512]),
                              deps=[t_mm] + B("t1").deps_write(), tag="e1ev")
                B("ps7").read(t_ev)
                B("t1").wrote(t_ev)
                for i in range(4):
                    t_cp = p.dma("act", e1z[i:i + 1, i * DM + half * 512: i * DM + (half + 1) * 512],
                                 t1[i:i + 1, 0:512], deps=[t_ev] + B("e1z").deps_write(),
                                 group="e1z", tag="e1cp")
                    B("t1").read(t_cp)
                    B("e1z").accum(t_cp)
            if part == 1:
                return
            e24_toks = []
            for half in range(2):
                t_mm = p.emit("pe", lambda e, half=half: e.matmul(
                    ps[7][0:4, :], e2s[:, :], lt[:, half * 512:(half + 1) * 512],
                    start=True, stop=True),
                    deps=lt_toks + B("ps7").deps_write() + B("e2s").deps_read(), tag="e2mm")
                B("ps7").wrote(t_mm)
                B("lt").read(t_mm)
                B("e2s").read(t_mm)
                t_ev = p.emit("act", lambda e, half=half: e.copy(
                    e24[:, half * 512:(half + 1) * 512], ps[7][0:4, 0:512]),
                    deps=[t_mm] + (B("u4").deps_write() if half == 0 else []), tag="e2ev")
                B("ps7").read(t_ev)
                if half == 0:
                    B("u4").wrote(t_ev)
                else:
                    B("u4").accum(t_ev)
                e24_toks.append(t_ev)
            t_out = p.dma("act", e2scr, e24[:, :], deps=e24_toks, group="e2b", tag="e2out")
            B("u4").read(t_out)
            ins = {}
            for i in range(4):
                for bb in range(2):
                    oc = 2 * i + bb
                    t_in = p.dma("act", e2st[:, oc * 4:(oc + 1) * 4],
                                 e2scr[i:i + 1, bb * 512:(bb + 1) * 512].rearrange(
                                     "o (p j) -> o p j", p=128)[0],
                                 deps=[t_out] + B("e2st").deps_write(), group=f"e2b{oc % 2}",
                                 tag="e2in")
                    B("e2st").wrote(t_in)
                    ins[oc] = t_in
            for g in range(2):
                trs = []
                for ocl in range(4):
                    oc = g * 4 + ocl
                    t_tr = p.emit("pe", lambda e, oc=oc, ocl=ocl: e.transpose(
                        p6b[0:4, ocl * 128:(ocl + 1) * 128], e2st[:, oc * 4:(oc + 1) * 4],
                        ident[:, :]),
                        deps=[ins[oc]] + B("ident").deps_read() +
                             (B("ps6").deps_write() if ocl == 0 else []), tag="e2tr")
                    B("e2st").read(t_tr)
                    trs.append(t_tr)
                B("ps6").wrote(trs[-1])
                t_ev = p.emit("act", lambda e, g=g: e.copy(
                    e2d[:, g * 512:(g + 1) * 512], p6b[0:4, 0:512]),
                    deps=trs + B("e2d").deps_write(), tag="e2dev")
                B("ps6").read(t_ev)
                if g == 0:
                    B("e2d").wrote(t_ev)
                else:
                    B("e2d").accum(t_ev)

        def t1_compute(t0):
            for nh in range(2):
                n0 = t0 + nh * 512
                last = None
                for cc in range(CC):
                    t_mm = p.emit("pe", lambda e, cc=cc, n0=n0: e.matmul(
                        ps[7][0:4, :], o1c[:, cc * 4:(cc + 1) * 4], xsl(cc, n0),
                        start=(cc == 0), stop=(cc == CC - 1)),
                        deps=B("xn").deps_read(rslot(cc, n0)) + B("o1c").deps_read() +
                             (B("ps7").deps_write() if cc == 0 else []), tag="t1mm")
                    B("xn").read(t_mm, rslot(cc, n0))
                    B("o1c").read(t_mm)
                    last = t_mm
                B("ps7").wrote(last)
                t_ev = p.emit("act", lambda e, nh=nh: e.copy(
                    t1[:, nh * 512:(nh + 1) * 512], ps[7][0:4, 0:512]),
                    deps=[last] + (B("t1").deps_write() if nh == 0 else []), tag="t1ev")
                B("ps7").read(t_ev)
                if nh == 0:
                    B("t1").wrote(t_ev)
                else:
                    B("t1").accum(t_ev)

        def ffn(store_seq=None, next_seq=None, hook=None, ln_pre=(), lora_pre=False, t1_pre=(), lora2_first_tile=False, prologue_pe_bcast=False):
            for ti in range(2):
                t0 = ti * NT
                if ti not in ln_pre:
                    layer_norm(t0, NT, False, bcast_pe=prologue_pe_bcast)
                if ti == 0 and prologue_pe_bcast:
                    layer_norm(NT, NT, False, bcast_pe=True)
                if ti == 0 and not lora_pre:
                    lora_prep(part=1 if lora2_first_tile else 0)
                if ti not in t1_pre:
                    t1_compute(t0)
                for g in range(32):
                    wt, wbn = next_w()
                    t_w = wslab("w1", g, wt, wbn)
                    for hcl in range(1):
                        hc = g
                        for nh in range(2):
                            n0 = t0 + nh * 512
                            bi4 = (hc % 2) * 2 + nh
                            bank, bn = ps[bi4], f"ps{bi4}"
                            for cc in range(CC):
                                t_mm = p.emit("pe", lambda e, cc=cc, n0=n0, wt=wt, bank=bank: e.matmul(
                                    bank[:, :], wt[:, cc * 128:(cc + 1) * 128],
                                    xsl(cc, n0), start=(cc == 0), stop=False),
                                    deps=[t_w] + B("xn").deps_read(rslot(cc, n0)) +
                                         (B(bn).deps_write() if cc == 0 else []), tag=f"h{hc}")
                                B(wbn).read(t_mm)
                                B("xn").read(t_mm, rslot(cc, n0))
                                if cc == 0:
                                    B(bn).wrote(t_mm)
                                else:
                                    B(bn).accum(t_mm)
                            t_c1 = p.emit("pe", lambda e, hc=hc, nh=nh, bank=bank: e.matmul(
                                bank[:, :], e1z[:, hc * 128:(hc + 1) * 128],
                                t1[:, nh * 512:(nh + 1) * 512], start=False, stop=True),
                                deps=B("e1z").deps_read() + B("t1").deps_read(), tag=f"c1_{hc}")
                            B("e1z").read(t_c1)
                            B("t1").read(t_c1)
                            B(bn).accum(t_c1)
                            t_g = p.emit("act", lambda e, hc=hc, nh=nh, bank=bank: e.activation(
                                h_act(hc, nh), bank[:, :], AF.Gelu, bias=b1c[:, hc:hc + 1]),
                                deps=[t_c1] + B("arena").deps_write(h_slot(hc)) + B("b1c").deps_read(),
                                tag=f"gelu{hc}")
                            B(bn).read(t_g)
                            B("b1c").read(t_g)
                            if nh == 0:
                                B("arena").wrote(t_g, h_slot(hc))
                            else:
                                B("arena").accum(t_g, h_slot(hc))
                if ti == 0 and lora2_first_tile:
                    lora_prep(part=2)
                for nh in range(2):
                    uev = []
                    for j in range(4):
                        bank, bn = ps[2 + j], f"ps{2 + j}"
                        last = None
                        for qc in range(CC):
                            hc = 8 * j + qc
                            t_mm = p.emit("pe", lambda e, hc=hc, qc=qc, nh=nh, bank=bank: e.matmul(
                                bank[0:4, :], o2c[:, qc * 4:(qc + 1) * 4], h_act(hc, nh),
                                start=(qc == 0), stop=(qc == CC - 1)),
                                deps=B("arena").deps_read(h_slot(hc)) + B("o2c").deps_read() +
                                     (B(bn).deps_write() if qc == 0 else []), tag=f"u{j}")
                            B("o2c").read(t_mm)
                            B("arena").read(t_mm, h_slot(hc))
                            last = t_mm
                        B(bn).wrote(last)
                        t_ev = p.emit("act", lambda e, j=j, bank=bank: e.copy(
                            u4[:, j * 512:(j + 1) * 512], bank[0:4, 0:512]),
                            deps=[last] + (B("u4").deps_write() if j == 0 else []), tag="uev")
                        B(bn).read(t_ev)
                        if j == 0:
                            B("u4").wrote(t_ev)
                        else:
                            B("u4").accum(t_ev)
                        uev.append(t_ev)
                    t_uo = p.dma("act", uscr, u4[:, :], deps=uev, group="ub", tag="u_out")
                    B("u4").read(t_uo)
                    t_uin = p.dma("act", u4[:, :].rearrange("j (i s) -> j i s", i=4),
                                  uscr.rearrange("i (j s) -> j i s", j=4),
                                  deps=[t_uo], group="ub", tag="u_in")
                    B("u4").wrote(t_uin)
                    for g in range(32):
                        wt, wbn = next_w()
                        t_w2l = wslab("w2", g, wt, wbn)
                        for oc in range(CC):
                            bank, bn = ps[oc], f"ps{oc}"
                            for hcl in range(1):
                                hc = g
                                first = (hc == 0)
                                t_mm = p.emit("pe", lambda e, oc=oc, hc=hc, nh=nh, wt=wt, bank=bank: e.matmul(
                                    bank[:, :], wt[:, oc * 128:(oc + 1) * 128],
                                    h_act(hc, nh), start=(hc == 0), stop=False),
                                    deps=[t_w2l] + B("arena").deps_read(h_slot(hc)) +
                                         (B(bn).deps_write() if first else []), tag=f"o{oc}")
                                B(wbn).read(t_mm)
                                B("arena").read(t_mm, h_slot(hc))
                                if first:
                                    B(bn).wrote(t_mm)
                                else:
                                    B(bn).accum(t_mm)
                    for oc in (6, 7, 0, 1, 2, 3, 4, 5):
                        bank, bn = ps[oc], f"ps{oc}"
                        i = oc // 2
                        t_c2 = p.emit("pe", lambda e, oc=oc, i=i, bank=bank: e.matmul(
                            bank[:, :], e2d[:, oc * 128:(oc + 1) * 128], u4[:, i * 512:(i + 1) * 512],
                            start=False, stop=True),
                            deps=[t_uin] + B("e2d").deps_read(), tag=f"c2_{oc}")
                        B("e2d").read(t_c2)
                        B("u4").read(t_c2)
                        B(bn).accum(t_c2)
                        t_t = p.emit("act", lambda e, oc=oc, bank=bank: e.add(
                            tmp_o[:, :], bank[:, :], b2c[:, oc:oc + 1]),
                            deps=[t_c2] + B("f32t").deps_write() + B("b2c").deps_read(),
                            tag=f"badd{oc}")
                        B(bn).read(t_t)
                        B("b2c").read(t_t)
                        B("f32t").wrote(t_t)
                        n0 = t0 + nh * 512
                        t_r = p.emit("dve", lambda e, oc=oc, n0=n0: e.scalar_tensor_tensor(
                            rsl(oc, n0), tmp_o[:, :], 0.5, rsl(oc, n0), ALU.mult, ALU.add),
                            deps=[t_t] + B("R").deps_write(rslot(oc, n0)), tag=f"radd{oc}")
                        B("f32t").read(t_r)
                        B("R").wrote(t_r, rslot(oc, n0))
                        if store_seq is not None:
                            t_st = p.dma("pool", outT[store_seq][oc][:, n0:n0 + 512],
                                         rsl(oc, n0), deps=[t_r], group=f"io{oc % 4}",
                                         tag=f"stR{store_seq}_{oc}_{n0}")
                            B("R").read(t_st, rslot(oc, n0))
                            st_toks.append(t_st)
                            if next_seq is not None:
                                t_ld = p.dma("pool", rsl(oc, n0), xT[next_seq][oc][:, n0:n0 + 512],
                                             deps=[t_st] + B("R").deps_write(rslot(oc, n0)),
                                             group=f"io{oc % 4}", tag=f"ldR{next_seq}_{oc}_{n0}")
                                B("R").wrote(t_ld, rslot(oc, n0))
                    if hook is not None:
                        hook(ti, nh)

        def attn(ln_pre_blocks=0, hook=None):
            for n0 in range(ln_pre_blocks * 512, S, 512):
                ln_block(n0, True)
            for hh in range(NHALF):
                h0 = hh * HPH
                def mhat_head(h):
                    for which, reg0, coloff in (("q", QT0, 0), ("k", KT0, 16)):
                        for nb in range(4):
                            col0 = reg0 + h * S + nb * 512
                            pr = nb % 2
                            t_sq = p.emit("pool", lambda e, col0=col0, pr=pr: e.tensor_mul(
                                sqt[:, pr * 512:(pr + 1) * 512], arena[:, col0:col0 + 512],
                                arena[:, col0:col0 + 512]),
                                deps=B("arena").deps_read(aslot(col0, 512)) + B("sqt").deps_write(pr),
                                tag="nsq")
                            B("sqt").wrote(t_sq, pr)
                            B("arena").read(t_sq, aslot(col0, 512))
                            t_mm = p.emit("pe", lambda e, pr=pr: e.matmul(
                                ps[6][0:1, :], onesc[:, :], sqt[:, pr * 512:(pr + 1) * 512],
                                start=True, stop=True),
                                deps=[t_sq] + B("ps6").deps_write(), tag="nmm")
                            B("ps6").wrote(t_mm)
                            B("sqt").read(t_mm, pr)
                            ncol = coloff + h * 4 + nb
                            t_rm = p.emit("dve", lambda e, ncol=ncol: e.reduce_max(
                                nrm[0:1, ncol:ncol + 1], ps[6][0:1, :], AX.X),
                                deps=[t_mm] + B("nrm").deps_write(), tag="nmax")
                            B("ps6").read(t_rm)
                            B("nrm").accum(t_rm)
                    t_q = p.emit("dve", lambda e, h=h: e.reduce_max(
                        nrm[0:1, 32 + h:33 + h], nrm[0:1, h * 4:(h + 1) * 4], AX.X),
                        deps=B("nrm").deps_read(), tag="nq")
                    B("nrm").accum(t_q)
                    t_k = p.emit("dve", lambda e, h=h: e.reduce_max(
                        nrm[0:1, 40 + h:41 + h], nrm[0:1, 16 + h * 4: 16 + (h + 1) * 4], AX.X),
                        deps=B("nrm").deps_read(), tag="nk")
                    B("nrm").accum(t_k)
                    t_pr = p.emit("dve", lambda e, h=h: e.tensor_mul(
                        nrm[0:1, 48 + h:49 + h], nrm[0:1, 32 + h:33 + h], nrm[0:1, 40 + h:41 + h]),
                        deps=[t_q, t_k], tag="nprod")
                    B("nrm").accum(t_pr)
                    t_sr = p.emit("act", lambda e, h=h: e.activation(
                        nrm[0:1, 48 + h:49 + h], nrm[0:1, 48 + h:49 + h], AF.Sqrt),
                        deps=[t_pr], tag="nsqrt")
                    B("nrm").accum(t_sr)
                    t_nm = p.emit("dve", lambda e, h=h: e.tensor_scalar(
                        nrm[0:1, 56 + h:57 + h], nrm[0:1, 48 + h:49 + h], -1.01, float(CUSH),
                        ALU.mult, ALU.add),
                        deps=[t_sr], tag="negm")
                    B("nrm").accum(t_nm)
                    t_no = p.dma("act", browN[0:1, h:h + 1], nrm[0:1, 56 + h:57 + h],
                                 deps=[t_nm], group="browN", tag="brNo")
                    B("nrm").read(t_no)
                    t_ev = p.dma("act", negm[:, h:h + 1],
                                 browN[0:1, h:h + 1].broadcast_to([128, 1]),
                                 deps=[t_no] + B("negm").deps_write(h), group="browN", tag="brNi")
                    B("negm").wrote(t_ev, h)
                def qkv_slab(kind, hh, col0):
                    gate = prep_done["wq"][1]
                    f0 = {"wq": 0, "wk": 1, "wv": 2}[kind] * DM + hh * 512
                    src = wqs[:, f0:f0 + 512].rearrange("(cc p) f -> p cc f", p=128)
                    dst = arena[:, col0:col0 + 4096].rearrange("p (cc f) -> p cc f", cc=CC)
                    sl = aslot(col0, 4096)
                    t = p.dma("sp", dst, src, deps=B("arena").deps_write(sl) + [gate],
                              group="wA", tag=f"{kind}{hh}")
                    B("arena").wrote(t, sl)
                    return t, sl

                t_wv, wv_sl = qkv_slab("wv", hh, QT0)
                for tt in range(16):
                    bank, bn = ps[tt % 2], f"ps{tt % 2}"
                    last = None
                    for cc in range(CC):
                        t_mm = p.emit("pe", lambda e, cc=cc, tt=tt, bank=bank: e.matmul(
                            bank[:, :], xn[:, cc * S + tt * 128: cc * S + (tt + 1) * 128],
                            arena[:, QT0 + cc * 512: QT0 + (cc + 1) * 512],
                            start=(cc == 0), stop=(cc == CC - 1)),
                            deps=[t_wv] + B("xn").deps_read(rslot(cc, tt * 128)) +
                                 (B(bn).deps_write() if cc == 0 else []), tag="wvmm")
                        B("arena").read(t_mm, wv_sl)
                        B("xn").read(t_mm, rslot(cc, tt * 128))
                        if cc == 0:
                            B(bn).wrote(t_mm)
                        else:
                            B(bn).accum(t_mm)
                        last = t_mm
                    col0 = V0 + tt * 512
                    slots = aslot(col0, 512)
                    t_ev = p.emit("dve", lambda e, col0=col0, bank=bank: e.tensor_copy(
                        arena[:, col0:col0 + 512], bank[:, :]),
                        deps=[last] + B("arena").deps_write(slots), tag="wvev")
                    B(bn).read(t_ev)
                    B("arena").wrote(t_ev, slots)
                def qk_head(h):
                    for kind, reg0, scale_ev in (("wq", QT0, SCALE_ATTN),
                                                 ("wk", KT0, None)):
                        wt, wbn = next_w()
                        wtok = wslab(kind, hh, wt, wbn, part=h)
                        for nb in range(4):
                            bank, bn = ps[nb % 2], f"ps{nb % 2}"
                            last = None
                            for cc in range(CC):
                                t_mm = p.emit("pe", lambda e, cc=cc, nb=nb, wt=wt, bank=bank: e.matmul(
                                    bank[:, :], wt[:, cc * 128:(cc + 1) * 128],
                                    xsl(cc, nb * 512), start=(cc == 0), stop=(cc == CC - 1)),
                                    deps=[wtok] + B("xn").deps_read(rslot(cc, nb * 512)) +
                                         (B(bn).deps_write() if cc == 0 else []), tag=kind + "mm")
                                B(wbn).read(t_mm)
                                B("xn").read(t_mm, rslot(cc, nb * 512))
                                if cc == 0:
                                    B(bn).wrote(t_mm)
                                else:
                                    B(bn).accum(t_mm)
                                last = t_mm
                            col0 = reg0 + h * S + nb * 512
                            slots = aslot(col0, 512)
                            if scale_ev is None:
                                t_ev = p.emit("act", lambda e, col0=col0, bank=bank: e.copy(
                                    arena[:, col0:col0 + 512], bank[:, :]),
                                    deps=[last] + B("arena").deps_write(slots), tag=kind + "ev")
                            else:
                                t_ev = p.emit("act", lambda e, col0=col0, bank=bank, scale_ev=scale_ev: e.mul(
                                    arena[:, col0:col0 + 512], bank[:, :], float(scale_ev)),
                                    deps=[last] + B("arena").deps_write(slots), tag=kind + "ev")
                            B(bn).read(t_ev)
                            B("arena").wrote(t_ev, slots)
                units = [(h, qb) for h in range(HPH) for qb in range(4)]
                gkt = [0]

                def emit_S_exp(h, qb, kt):
                    sb_i = (0, 1, 7)[gkt[0] % 3]
                    bank, bn = ps[sb_i], f"ps{sb_i}"
                    pslot = gkt[0] % 4
                    gkt[0] += 1
                    diag = (kt // 4 == qb)
                    dep0 = B(bn).deps_write()
                    if diag:
                        m = kt - 4 * qb
                        t_pl = p.emit("pe", lambda e, m=m, bank=bank: e.matmul(
                            bank[:, :], ident[:, :], maskm[:, 384 - 128 * m: 896 - 128 * m],
                            start=True, stop=False),
                            deps=dep0 + B("maskm").deps_read() + B("ident").deps_read(),
                            tag="spl")
                        B(bn).wrote(t_pl)
                        B("maskm").read(t_pl)
                        B("ident").read(t_pl)
                        dep0 = []
                    kcol = KT0 + h * S + kt * 128
                    qcol = QT0 + h * S + qb * 512
                    t_s = p.emit("pe", lambda e, kcol=kcol, qcol=qcol, bank=bank, diag=diag: e.matmul(
                        bank[:, :], arena[:, kcol:kcol + 128], arena[:, qcol:qcol + 512],
                        start=not diag, stop=True),
                        deps=dep0 + B("arena").deps_read(aslot(kcol, 128) + aslot(qcol, 512)),
                        tag="smm")
                    if diag:
                        B(bn).accum(t_s)
                    else:
                        B(bn).wrote(t_s)
                    B("arena").read(t_s, aslot(kcol, 128) + aslot(qcol, 512))
                    t_e = p.emit("act", lambda e, pslot=pslot, h=h, bank=bank: e.activation(
                        pb[:, pslot * 512:(pslot + 1) * 512], bank[:, :], AF.Exp,
                        bias=negm[:, h:h + 1]),
                        deps=[t_s] + B("pb").deps_write(pslot) + B("negm").deps_read(h),
                        tag="exp")
                    B(bn).read(t_e)
                    B("negm").read(t_e, h)
                    B("pb").wrote(t_e, pslot)
                    return t_e, pslot

                def emit_rspv(h, qb, kt, t_e, pslot, first, last, ubank):
                    rsb, rsn = (ps[3], "ps3") if ubank == 0 else (ps[6], "ps6")
                    ob, obn = (ps[2], "ps2") if ubank == 0 else (ps[5], "ps5")
                    t_rs = p.emit("pe", lambda e, pslot=pslot, first=first, last=last, rsb=rsb: e.matmul(
                        rsb[0:1, :], onesc[:, :], pb[:, pslot * 512:(pslot + 1) * 512],
                        start=first, stop=last),
                        deps=[t_e] + (B(rsn).deps_write() if first else []), tag="rsmm")
                    B("pb").read(t_rs, pslot)
                    if first:
                        B(rsn).wrote(t_rs)
                    else:
                        B(rsn).accum(t_rs)
                    vcol = V0 + kt * 512 + h * 128
                    t_pv = p.emit("pe", lambda e, vcol=vcol, pslot=pslot, first=first, last=last, ob=ob: e.matmul(
                        ob[:, :], arena[:, vcol:vcol + 128], pb[:, pslot * 512:(pslot + 1) * 512],
                        start=first, stop=last),
                        deps=[t_e] + B("arena").deps_read(aslot(vcol, 128)) +
                             (B(obn).deps_write() if first else []), tag="pvmm")
                    B("pb").read(t_pv, pslot)
                    B("arena").read(t_pv, aslot(vcol, 128))
                    if first:
                        B(obn).wrote(t_pv)
                    else:
                        B(obn).accum(t_pv)

                def emit_tail(h, qb, ubank):
                    rsb, rsn = (ps[3], "ps3") if ubank == 0 else (ps[6], "ps6")
                    ob, obn = (ps[2], "ps2") if ubank == 0 else (ps[5], "ps5")
                    t_l = p.emit("act", lambda e, rsb=rsb: e.activation(recr[:, :], rsb[0:1, :], AF.Ln),
                                 deps=B(rsn).deps_read() + B("srow_a").deps_write(), tag="rln")
                    B(rsn).read(t_l)
                    B("srow_a").wrote(t_l)
                    t_x = p.emit("act", lambda e: e.activation(recr[:, :], recr[:, :], AF.Exp, scale=-1.0),
                                 deps=[t_l], tag="rexp")
                    B("srow_a").wrote(t_x)
                    t_ro = p.dma("act", browF, recr[:, :], deps=[t_x], group="browF", tag="brFo")
                    B("srow_a").read(t_ro)
                    t_rb2 = p.dma("act", f32t[:, :], browF[0:1, :].broadcast_to([128, 512]),
                                  deps=[t_ro] + B("f32t").deps_write(), group="browF", tag="brFi")
                    B("f32t").wrote(t_rb2)
                    ocol = OA0 + h * S + qb * 512
                    t_oa = p.emit("dve", lambda e, ocol=ocol, ob=ob: e.tensor_mul(
                        arena[:, ocol:ocol + 512], ob[:, :], f32t[:, :]),
                        deps=[t_rb2] + B(obn).deps_read() + B("arena").deps_write(aslot(ocol, 512)),
                        tag="oaev")
                    B(obn).read(t_oa)
                    B("f32t").read(t_oa)
                    B("arena").wrote(t_oa, aslot(ocol, 512))

                pending = []
                pre = {}
                DEPTH = 2
                fstate = {"i": 0}
                qk_done = set()

                def emit_flash_unit():
                    ui = fstate["i"]
                    fstate["i"] += 1
                    h, qb = units[ui]
                    nkt = qb * 4 + 4
                    ubank = ui % 2

                    def push(item):
                        pending.append(item)
                        if len(pending) > DEPTH:
                            emit_rspv(*pending.pop(0))

                    done = pre.pop((h, qb), [])
                    for kt, t_e, pslot in done:
                        push((h, qb, kt, t_e, pslot, kt == 0, kt == nkt - 1, ubank))
                    for kt in range(len(done), nkt):
                        t_e, pslot = emit_S_exp(h, qb, kt)
                        push((h, qb, kt, t_e, pslot, kt == 0, kt == nkt - 1, ubank))
                    if ui + 1 < len(units) and units[ui + 1][0] in qk_done:
                        nh_, nqb_ = units[ui + 1]
                        lst = []
                        for kt2 in range(min(2, nqb_ * 4 + 4)):
                            te2, psl2 = emit_S_exp(nh_, nqb_, kt2)
                            lst.append((kt2, te2, psl2))
                        pre[(nh_, nqb_)] = lst
                    while pending:
                        emit_rspv(*pending.pop(0))
                    emit_tail(h, qb, ubank)

                for h in range(HPH):
                    qk_head(h)
                    qk_done.add(h)
                    mhat_head(h)
                    if h >= 1:
                        for _ in range(4):
                            emit_flash_unit()
                for _ in range(4):
                    emit_flash_unit()
                gate = prep_done["wo"][hh]
                src = wos[hh * 512:(hh + 1) * 512, :].rearrange("(fc p) c -> p fc c", p=128)
                wo_sl = aslot(KT0, 4096)
                dst = arena[:, KT0:KT0 + 4096].rearrange("p (fc c) -> p fc c", fc=4)
                t_wo = p.dma("sp", dst, src, deps=B("arena").deps_write(wo_sl) + [gate],
                             group="wA", tag=f"wo{hh}")
                B("arena").wrote(t_wo, wo_sl)
                for qb in range(4):
                    for oc in range(CC):
                        bank, bn = ps[5 + (oc % 3)], f"ps{5 + (oc % 3)}"
                        last = None
                        for fc in range(HPH):
                            ocol = OA0 + fc * S + qb * 512
                            t_mm = p.emit("pe", lambda e, fc=fc, oc=oc, ocol=ocol, bank=bank: e.matmul(
                                bank[:, :], arena[:, KT0 + fc * 1024 + oc * 128: KT0 + fc * 1024 + (oc + 1) * 128],
                                arena[:, ocol:ocol + 512], start=(fc == 0), stop=(fc == HPH - 1)),
                                deps=[t_wo] + B("arena").deps_read(aslot(ocol, 512)) +
                                     (B(bn).deps_write() if fc == 0 else []), tag="opmm")
                            B("arena").read(t_mm, wo_sl)
                            B("arena").read(t_mm, aslot(ocol, 512))
                            if fc == 0:
                                B(bn).wrote(t_mm)
                            else:
                                B(bn).accum(t_mm)
                            last = t_mm
                        n0 = qb * 512
                        t_r = p.emit("dve", lambda e, oc=oc, n0=n0, bank=bank: e.tensor_add(
                            rsl(oc, n0), rsl(oc, n0), bank[:, :]),
                            deps=[last] + B("R").deps_write(rslot(oc, n0)), tag="raddA")
                        B(bn).read(t_r)
                        B("R").wrote(t_r, rslot(oc, n0))
                    if hook is not None:
                        hook(hh, qb)

        def load_R_blocks(seq, b0, b1):
            for cc in range(CC):
                sl = [cc * 4 + b for b in range(b0, b1)]
                t = p.dma("sp", R[:, cc * S + b0 * 512: cc * S + b1 * 512],
                          xT[seq][cc][:, b0 * 512: b1 * 512],
                          deps=B("R").deps_write(sl), group=f"ioA{cc % 4}",
                          tag=f"ldR{seq}_{cc}_{b0}")
                B("R").wrote(t, sl)

        def store_R(seq):
            dst = outT[seq].rearrange("cc p s -> p cc s")
            src = R[:, :].rearrange("p (cc s) -> p cc s", cc=CC)
            t = p.dma("pool", dst, src, deps=B("R").deps_read() + B("R").deps_write(),
                      group="io0", tag=f"stR{seq}")
            B("R").read(t)
            return t

        st_toks = []
        _blocks = do_prep()
        load_R_blocks(0, 0, 1)
        load_R_blocks(0, 1, 2)
        load_R_blocks(0, 2, 4)
        emit_prep([b for b in _blocks if b[0] == "w1"])
        emit_prep([b for b in _blocks if b[0] != "w1"])
        for seq in range(BPC):
            if n_sub >= 3:
                first = (seq == 0)

                def ffn1_hook(ti, nh):
                    if ti == 0 and nh == 0:
                        if not first:
                            ln_block(NT, False, sq_eng='pool')
                            ln_block(NT + 512, False, sq_eng='pool')
                        t1_compute(NT)
                    elif ti == 1 and nh == 0:
                        ln_block(0, True, sq_eng='pool')
                        ln_block(512, True, sq_eng='pool')
                        ln_block(NT, True, sq_eng='pool')
                    elif ti == 1 and nh == 1:
                        ln_block(NT + 512, True, sq_eng='pool')

                ffn(hook=ffn1_hook, ln_pre=(1,) if first else (0, 1),
                    lora_pre=not first, t1_pre=(1,) if first else (0, 1),
                    lora2_first_tile=True, prologue_pe_bcast=first)

                def attn_hook(hh, qb):
                    if hh != NHALF - 1:
                        return
                    if qb == 0:
                        ln_block(0, False, sq_eng='pool')
                    elif qb == 1:
                        ln_block(512, False, sq_eng='pool')
                    elif qb == 2:
                        lora_prep(part=1)
                        ln_block(NT, False, sq_eng='pool')
                    elif qb == 3:
                        t1_compute(0)
                        ln_block(NT + 512, False, sq_eng='pool')

                nxt = seq + 1 if seq + 1 < BPC else None

                def ffn2_hook(ti, nh):
                    if ti == 0 and nh == 0:
                        t1_compute(NT)
                    if nxt is not None:
                        if ti == 0 and nh == 1:
                            ln_block(0, False, sq_eng='pool')
                        elif ti == 1 and nh == 0:
                            ln_block(512, False, sq_eng='pool')
                            lora_prep(part=1)
                            t1_compute(0)

                attn(ln_pre_blocks=4, hook=attn_hook)
                ffn(store_seq=seq, next_seq=nxt, hook=ffn2_hook,
                    ln_pre=(0, 1), lora_pre=True, t1_pre=(1,), lora2_first_tile=True)
            else:
                if seq > 0:
                    load_R(seq)
                if n_sub >= 1:
                    ffn()
                if n_sub >= 2:
                    attn()
                st_toks.append(store_R(seq))
        p.emit("pool", lambda e: e.engine_nop(), deps=st_toks, tag="final_wait")
        p.emit("act", lambda e: e.mul(srow_c[0:1, 0:1], srow_c[0:1, 0:1], 1.0),
               deps=B("srow_c").deps_write(), tag="trail_nop_act")
        p.emit("dve", lambda e: e.engine_nop(), deps=[], tag="trail_nop_dve")

        stats = p.finalize_and_emit(ctx)
        stats["sbuf_left"] = sbuf_left
    return nc, stats



def shard_inputs(L0_attn_x, L0_attn_gamma, L0_attn_w_qkv, L0_attn_w_out,
                 L0_ffn_W1, L0_ffn_b1, L0_ffn_W2, L0_ffn_b2, L0_ffn_scale):
    f32 = np.float32
    com = {
        "w1T": np.ascontiguousarray(np.asarray(L0_ffn_W1, f32).T),
        "w2T": np.ascontiguousarray(np.asarray(L0_ffn_W2, f32).T),
        "wqkvT": np.ascontiguousarray(np.asarray(L0_attn_w_qkv, f32).T),
        "woutT": np.ascontiguousarray(np.asarray(L0_attn_w_out, f32).T),
        "b1h": np.ascontiguousarray(np.asarray(L0_ffn_b1, f32).reshape(HC, 128).T),
        "b2h": np.ascontiguousarray(np.asarray(L0_ffn_b2, f32).reshape(CC, 128).T),
        "gamh": np.ascontiguousarray(np.asarray(L0_attn_gamma, f32).reshape(CC, 128).T),
        "scaleh": np.asarray(L0_ffn_scale, f32).reshape(1, 1),
    }
    x = np.asarray(L0_attn_x, f32)
    maps = []
    for core in range(x.shape[0] // BPC):
        xs = x[core * BPC:(core + 1) * BPC]
        xTc = np.ascontiguousarray(xs.transpose(0, 2, 1)).reshape(BPC, CC, 128, S)
        maps.append({**com, "xT": xTc})
    return maps


def gather_outputs(results):
    outs = []
    for r in results:
        o = np.asarray(r["outT"]).reshape(BPC, DM, S).transpose(0, 2, 1)
        outs.append(o)
    return np.ascontiguousarray(np.concatenate(outs, axis=0))



from concourse.bass_utils import run_bass_kernel_spmd


def kernel(**inputs):
    maps = shard_inputs(**inputs)
    nc, _stats = build(3)
    res = run_bass_kernel_spmd(nc, maps, core_ids=list(range(8)))
    return gather_outputs(res.results)


# version 25
# speedup vs baseline: 1.0191x; 1.0191x over previous
import contextlib
import numpy as np
import ml_dtypes
import concourse.bass as bass
import concourse.mybir as mybir

F32 = mybir.dt.float32
BF16 = mybir.dt.bfloat16
AF = mybir.ActivationFunctionType
ALU = mybir.AluOpType
AX = mybir.AxisListType
bfnp = ml_dtypes.bfloat16

ENGINES = ("pe", "act", "dve", "pool", "sp")


class Tok:
    __slots__ = ("eng", "idx", "sem", "val")

    def __init__(self, eng, idx):
        self.eng, self.idx, self.sem, self.val = eng, idx, None, None


class Op:
    __slots__ = ("fn", "deps", "tok", "dma_group", "needs_inc", "tag", "gseq")

    def __init__(self, fn, deps, tok, dma_group, tag, gseq):
        self.fn, self.deps, self.tok = fn, deps, tok
        self.dma_group, self.needs_inc, self.tag = dma_group, False, tag
        self.gseq = gseq


class Prog:
    def __init__(self, nc):
        self.nc = nc
        self.ops = {e: [] for e in ENGINES}
        self.n_emitted = 0
        self.last_dma = {}
        self.group_mode = {}

    def emit(self, eng, fn, deps=(), dma_group=None, tag=""):
        dl = []
        seen = {}
        for d in deps:
            if d is None:
                continue
            m = False
            if isinstance(d, tuple):
                m = True
                d = d[1]
            k = (d.eng, d.idx)
            if k in seen:
                if m and not seen[k][1]:
                    seen[k][0][:] = [d, True]
                continue
            ent = [d, m]
            seen[k] = (ent, m)
            dl.append(ent)
        tok = Tok(eng, len(self.ops[eng]))
        self.ops[eng].append(Op(fn, dl, tok, dma_group, tag, self.n_emitted))
        self.n_emitted += 1
        return tok

    def dma(self, eng, out_ap, in_ap, deps=(), group=None, serial=True, tag="", **kw):
        assert group is not None
        mode = (eng, "serial" if serial else "batch")
        assert self.group_mode.setdefault(group, mode) == mode, (group, mode)

        def fn(e):
            return e.dma_start(out_ap, in_ap, **kw)

        deps = list(deps)
        if serial and group in self.last_dma:
            deps.append(self.last_dma[group])
        tok = self.emit(eng, fn, deps=deps, dma_group=group, tag=tag)
        self.last_dma[group] = tok
        return tok

    def finalize_and_emit(self, ctx):
        nc = self.nc

        def elided(e, d):
            dop = self.ops[d.eng][d.idx]
            return e == "pe" and d.eng == "pe" and dop.dma_group is None

        def margin_tok(d, consumer):
            lst = self.ops[d.eng]
            j = d.idx + 1
            if j < len(lst) and lst[j].gseq < consumer.gseq:
                return lst[j]
            return None

        for e in ENGINES:
            for op in self.ops[e]:
                if op.dma_group is not None:
                    op.needs_inc = True
                for d, m in op.deps:
                    if elided(e, d):
                        continue
                    self.ops[d.eng][d.idx].needs_inc = True
                    if m:
                        mo = margin_tok(d, op)
                        if mo is not None and mo.dma_group is None:
                            mo.needs_inc = True
        sems = {}

        def get_sem(name):
            if name not in sems:
                sems[name] = ctx.enter_context(nc.semaphore(name=name))
            return sems[name]

        counts = {}
        for e in ENGINES:
            for op in self.ops[e]:
                if not op.needs_inc:
                    continue
                if op.dma_group is not None:
                    sname, amt = "d_" + op.dma_group, 16
                else:
                    sname, amt = "c_" + e, 1
                counts[sname] = counts.get(sname, 0) + amt
                op.tok.sem, op.tok.val = sname, counts[sname]
        stats = {"waits": 0, "incs": 0, "sems": len(counts), "ops": self.n_emitted}

        def make_runner(e):
            oplist = self.ops[e]

            def run(engine_obj):
                hiwater = {}
                for op in oplist:
                    waits = {}
                    for d, m in op.deps:
                        if elided(e, d):
                            continue
                        s, v = d.sem, d.val
                        if m:
                            mo = margin_tok(d, op)
                            if mo is not None and mo.dma_group is None:
                                s, v = mo.tok.sem, mo.tok.val
                        assert s is not None, (e, op.tag, d.eng, d.idx)
                        if hiwater.get(s, 0) >= v:
                            continue
                        if waits.get(s, 0) < v:
                            waits[s] = v
                    for s, v in waits.items():
                        engine_obj.wait_ge(get_sem(s), v)
                        hiwater[s] = v
                        stats["waits"] += 1
                    ins = op.fn(engine_obj)
                    if op.needs_inc:
                        amt = 16 if op.dma_group is not None else 1
                        ins.then_inc(get_sem(op.tok.sem), amt)
                        stats["incs"] += 1

            return run

        with nc.Block() as block:
            block.tensor(make_runner("pe"))
            block.scalar(make_runner("act"))
            block.vector(make_runner("dve"))
            block.gpsimd(make_runner("pool"))
            block.sync(make_runner("sp"))
        return stats


class Buf:

    def __init__(self, nslots=1, margin=False):
        self.n = nslots
        self.margin = margin
        self.w = [[] for _ in range(nslots)]
        self.r = [[] for _ in range(nslots)]

    def _sl(self, s):
        if s is None:
            return range(self.n)
        if isinstance(s, int):
            return (s,)
        return s

    def deps_write(self, s=None):
        out = []
        for i in self._sl(s):
            out += self.w[i]
            if self.margin:
                out += [("m", t) for t in self.r[i]]
            else:
                out += self.r[i]
        return out

    def deps_read(self, s=None):
        out = []
        for i in self._sl(s):
            out += self.w[i]
        return out

    def wrote(self, tok, s=None):
        for i in self._sl(s):
            self.w[i] = [tok]
            self.r[i] = []

    def accum(self, tok, s=None):
        for i in self._sl(s):
            self.w[i].append(tok)

    def read(self, tok, s=None):
        for i in self._sl(s):
            self.r[i].append(tok)


DM = 1024
S = 2048
BPC = 2
H = 4096
CC = 8
HC = 32
HEADS = 8
DH = 128
NHALF = 2
HPH = 4
SCALE_ATTN = DH ** -0.5
CUSH = 30.0
NT = 1024
LN_EPS = 1e-5
PSUM_MARGIN = False


def host_constants():
    c = {}
    c["ones_col_bf"] = np.ones((128, 1), bfnp)
    c["ones_col_f32"] = np.ones((128, 1), np.float32)
    c["ones_row_f32"] = np.ones((1, 128), np.float32)
    c["negones_row_f32"] = -np.ones((1, 128), np.float32)
    c["ident_bf"] = np.eye(128, dtype=bfnp)
    j = np.arange(896)[None, :] - 384.0
    k = np.arange(128)[:, None]
    c["mask_master_bf"] = np.where(j >= k, 0.0, -1e30).astype(bfnp)

    def sel(rows):
        m = np.zeros((16, 4), np.float32)
        for i, r in enumerate(rows):
            m[r, i] = 1.0
        return m.astype(bfnp)

    c["o1sel_bf"] = sel([1, 3, 5, 7])
    c["o2sel_bf"] = sel([9, 11, 13, 15])
    c["e1sel_bf"] = sel([0, 2, 4, 6])
    c["e2sel_bf"] = sel([8, 10, 12, 14])
    return c


def build(n_sub=3):
    nc = bass.Bass("TRN2", target_bir_lowering=False, debug=False)
    C = host_constants()

    xT = nc.dram_tensor("xT", [BPC, CC, 128, S], F32, kind="ExternalInput").ap()
    outT = nc.dram_tensor("outT", [BPC, CC, 128, S], F32, kind="ExternalOutput").ap()
    w1T = nc.dram_tensor("w1T", [DM, H], F32, kind="ExternalInput").ap()
    w2T = nc.dram_tensor("w2T", [H, DM], F32, kind="ExternalInput").ap()
    wqkvT = nc.dram_tensor("wqkvT", [DM, 3 * DM], F32, kind="ExternalInput").ap()
    woutT = nc.dram_tensor("woutT", [DM, DM], F32, kind="ExternalInput").ap()
    b1h = nc.dram_tensor("b1h", [128, HC], F32, kind="ExternalInput").ap()
    b2h = nc.dram_tensor("b2h", [128, CC], F32, kind="ExternalInput").ap()
    gamh = nc.dram_tensor("gamh", [128, CC], F32, kind="ExternalInput").ap()
    scaleh = nc.dram_tensor("scaleh", [1, 1], F32, kind="ExternalInput").ap()

    consts = {k: nc.inline_tensor(v, name=k).ap() for k, v in C.items()}
    e2scr = nc.dram_tensor("e2scr", [4, DM], BF16, kind="Internal").ap()
    w1s = nc.dram_tensor("w1s", [DM, H], BF16, kind="Internal").ap()
    w2s = nc.dram_tensor("w2s", [H, DM], BF16, kind="Internal").ap()
    wqs = nc.dram_tensor("wqs", [DM, 3 * DM], BF16, kind="Internal").ap()
    wos = nc.dram_tensor("wos", [DM, DM], BF16, kind="Internal").ap()
    uscr = nc.dram_tensor("uscr", [4, 4 * 512], BF16, kind="Internal").ap()
    browL = nc.dram_tensor("browL", [1, 1024], F32, kind="Internal").ap()
    browF = nc.dram_tensor("browF", [1, 512], F32, kind="Internal").ap()
    browN = nc.dram_tensor("browN", [1, 8], F32, kind="Internal").ap()

    p = Prog(nc)
    bufs = {}
    ctx = contextlib.ExitStack()
    with ctx:
        def sb(name, shape, dtype, nslots=1):
            t = ctx.enter_context(nc.sbuf_tensor(name, shape, dtype))
            bufs[name] = Buf(nslots)
            return t

        R = sb("R", [128, CC * S], F32, nslots=32)
        arena = sb("arena", [128, HC * NT], BF16, nslots=32)
        xn = sb("xn", [128, CC * S], BF16, nslots=32)
        wb0 = sb("wb0", [128, 1024], BF16)
        wb1 = sb("wb1", [128, 1024], BF16)
        wb2 = sb("wb2", [128, 1024], BF16)
        f32t = sb("f32t", [128, 512], F32)
        ln_t = f32t
        tmp_o = f32t
        recb = f32t
        sqt = sb("sqt", [128, 1024], BF16, nslots=2)
        qsq = sqt
        srow_a = sb("srow_a", [1, 512], F32)
        srow_b = sb("srow_b", [1, 512], F32)
        srow_c = sb("srow_c", [1, 512], F32)
        e1z = sb("e1z", [4, H], BF16)
        t1 = sb("t1", [4, NT], BF16)
        lt = sb("lt", [16, DM], BF16)

        e2d = sb("e2d", [4, CC * 128], BF16)
        u4 = sb("u4", [4, 4 * 512], BF16)
        e24 = u4[:, 0:DM]
        e2st = sb("e2st", [128, 32], BF16)
        pb = sb("pb", [128, 512 * 4], BF16, nslots=4)
        recr = srow_a
        rep_a = sb("rep_a", [128, 512], F32)
        rep_b = sb("rep_b", [128, 512], F32)
        nrm = sb("nrm", [1, 64], F32)
        negm = sb("negm", [128, HPH], F32, nslots=HPH)
        b1c = sb("b1c", [128, HC], F32)
        b2c = sb("b2c", [128, CC], F32)
        gamc = sb("gamc", [128, CC], F32)
        scol = sb("scol", [16, 1], F32)
        onesc = sb("onesc", [128, 1], BF16)
        onesr = sb("onesr", [1, 128], F32)
        negr = sb("negr", [1, 128], F32)
        onesf = sb("onesf", [128, 1], F32)
        ident = sb("ident", [128, 128], BF16)
        maskm = sb("maskm", [128, 896], BF16)
        o1s = sb("o1s", [16, 4], BF16)
        o2s = sb("o2s", [16, 4], BF16)
        e1s = sb("e1s", [16, 4], BF16)
        e2s = sb("e2s", [16, 4], BF16)
        o1sc = sb("o1sc", [16, 4], BF16)
        o2sc = sb("o2sc", [16, 4], BF16)
        o1c = sb("o1c", [128, CC * 4], BF16)
        o2c = sb("o2c", [128, CC * 4], BF16)
        sbuf_left = nc.sbuf_bytes_remaining
        ps = []
        for i in range(8):
            ps.append(ctx.enter_context(nc.psum_tensor(f"ps{i}", [128, 512], F32)))
            bufs[f"ps{i}"] = Buf(margin=PSUM_MARGIN)

        def B(name):
            return bufs[name]

        def rsl(cc, t0_, w=512):
            return R[:, cc * S + t0_: cc * S + t0_ + w]

        def rslot(cc, t0_):
            return cc * 4 + t0_ // 512

        def xsl(cc, t0_, w=512):
            return xn[:, cc * S + t0_: cc * S + t0_ + w]

        def h_act(hc, nh):
            return arena[:, hc * NT + nh * 512: hc * NT + (nh + 1) * 512]

        def h_slot(hc):
            return hc

        KT0, V0, QT0, OA0 = 0, HPH * S, 2 * HPH * S, 3 * HPH * S

        def aslot(col0, width):
            return list(range(col0 // NT, (col0 + width + NT - 1) // NT))

        ld = []
        for dst, src, tag in ((b1c, b1h, "b1"), (b2c, b2h, "b2"), (gamc, gamh, "gam"),
                              (onesc, consts["ones_col_bf"], "oc"),
                              (onesr, consts["ones_row_f32"], "or"),
                              (negr, consts["negones_row_f32"], "nr"),
                              (onesf, consts["ones_col_f32"], "of"),
                              (ident, consts["ident_bf"], "id"),
                              (maskm, consts["mask_master_bf"], "mask"),
                              (o1s, consts["o1sel_bf"], "o1s"),
                              (o2s, consts["o2sel_bf"], "o2s"),
                              (e1s, consts["e1sel_bf"], "e1s"),
                              (e2s, consts["e2sel_bf"], "e2s")):
            t = p.dma("sp", dst[:, :], src, group="ini", serial=False, tag=tag)
            ld.append(t)
        ini_tok = p.emit("dve", lambda e: e.engine_nop(), deps=ld, tag="ini_fence")
        for name in ("b1c", "b2c", "gamc", "onesc", "onesf", "onesr", "negr", "ident",
                     "maskm", "o1s", "o2s", "e1s", "e2s", "srow_a"):
            B(name).wrote(ini_tok)
        t_sc_ev = p.dma("sp", scol[:, :], scaleh[0:1, 0:1].broadcast_to([16, 1]),
                        deps=[ini_tok], group="scb", tag="scbc")
        B("scol").wrote(t_sc_ev)
        for src_s, dst_s, nm in ((o1s, o1sc, "o1sc"), (o2s, o2sc, "o2sc")):
            t = p.emit("dve", lambda e, src_s=src_s, dst_s=dst_s: e.tensor_scalar_mul(
                dst_s[:, :], src_s[:, :], scol[:, 0:1]),
                deps=[t_sc_ev, ini_tok], tag=nm)
            B(nm).wrote(t)
            B("scol").read(t)
        t_ms = p.emit("dve", lambda e: e.memset(e1z[:, :], 0.0), deps=[], tag="e1z_memset")
        B("e1z").wrote(t_ms)

        warm_tok = ini_tok
        for wi in range(40):
            warm_tok = p.emit("pe", lambda e: e.matmul(
                ps[7][0:1, 0:512], onesc[:, :], maskm[:, 0:512],
                start=True, stop=True), deps=[warm_tok], tag=f"warm{wi}")

        prep_done = {}

        def do_prep():
            blocks = []
            for b in range(4):
                src = w1T[:, b * 1024:(b + 1) * 1024].rearrange("(g p) h -> p g h", p=128)
                dst = w1s[:, b * 1024:(b + 1) * 1024].rearrange("(g p) h -> p g h", p=128)
                blocks.append(("w1", b, src, dst, 8192))
            for b in range(4):
                src = w2T[b * 1024:(b + 1) * 1024, :].rearrange("(g p) o -> p g o", p=128)
                dst = w2s[b * 1024:(b + 1) * 1024, :].rearrange("(g p) o -> p g o", p=128)
                blocks.append(("w2", b, src, dst, 8192))
            for b in range(4):
                src = wqkvT[b * 256:(b + 1) * 256, :].rearrange("(g p) f -> p g f", p=128)
                dst = wqs[b * 256:(b + 1) * 256, :].rearrange("(g p) f -> p g f", p=128)
                blocks.append(("wq", b, src, dst, 6144))
            for b in range(2):
                src = woutT[b * 512:(b + 1) * 512, :].rearrange("(g p) c -> p g c", p=128)
                dst = wos[b * 512:(b + 1) * 512, :].rearrange("(g p) c -> p g c", p=128)
                blocks.append(("wo", b, src, dst, 4096))
            for kind in ("w1", "w2", "wq", "wo"):
                prep_done[kind] = {}
            return blocks

        def emit_prep(blocks):
            for bi, (kind, b, src, dst, width) in enumerate(blocks):
                par = bi % 2
                slot0 = par * 8
                stg = arena[:, slot0 * NT: slot0 * NT + width]
                sl = list(range(slot0, slot0 + 8))
                stg3 = stg.rearrange("p (g x) -> p g x", x=src.shape[2])
                t_l = p.dma("pool", stg3, src, deps=B("arena").deps_write(sl),
                            group=f"prep_l{par}", tag=f"pl_{kind}{b}")
                B("arena").wrote(t_l, sl)
                t_s = p.dma("sp", dst, stg3, deps=[t_l], group=f"prep_s{par}",
                            tag=f"ps_{kind}{b}")
                B("arena").read(t_s, sl)
                prep_done[kind][b] = t_s
            prep_done["wk"] = prep_done["wv"] = prep_done["wq"]

        def wslab(kind, arg, wt, wbname, part=0):
            if kind == "w1":
                gate = prep_done["w1"][arg // 8]
                src = w1s[:, arg * 128:(arg + 1) * 128].rearrange("(cc p) h -> p cc h", p=128)
                dst = wt[:, :].rearrange("p (cc h) -> p cc h", cc=CC)
            elif kind == "w2":
                gate = prep_done["w2"][arg // 8]
                src = w2s[arg * 128:(arg + 1) * 128, :].rearrange("(hh p) o -> p hh o", p=128)
                dst = wt[:, :].rearrange("p (hh o) -> p hh o", hh=1)
            elif kind in ("wq", "wk"):
                gate = prep_done["wq"][3]
                f0 = {"wq": 0, "wk": 1}[kind] * DM + arg * 512 + part * 128
                src = wqs[:, f0:f0 + 128].rearrange("(cc p) f -> p cc f", p=128)
                dst = wt[:, :].rearrange("p (cc f) -> p cc f", cc=CC)
            else:
                gate = prep_done["wo"][arg]
                r0 = arg * 512 + part * 128
                src = wos[r0:r0 + 128, :].rearrange("(fc p) c -> p fc c", p=128)
                dst = wt[:, :].rearrange("p (fc c) -> p fc c", fc=1)
            deps = B(wbname).deps_write() + [gate]
            grp = {"wb0": "wA", "wb1": "wB", "wb2": "wC"}[wbname]
            t = p.dma("sp", dst, src, deps=deps, group=grp, tag=f"{kind}{arg}.{part}")
            B(wbname).wrote(t)
            return t

        wrot = [0]
        _wbs = ((wb0, "wb0"), (wb1, "wb1"), (wb2, "wb2"))

        def next_w():
            i = wrot[0] % 3
            wrot[0] += 1
            return _wbs[i]

        def layer_norm(t0, ntok, gamma_fused, bcast_pe=False, sq_eng="act"):
            for n0 in range(t0, t0 + ntok, 512):
                ln_block(n0, gamma_fused, bcast_pe, sq_eng)

        def ln_block(n0, gamma_fused, bcast_pe=False, sq_eng="act"):
            if True:
                ta = tb = None
                for cc in range(CC):
                    t_sa = p.emit("pe", lambda e, cc=cc, n0=n0: e.matmul(
                        ps[6][0:1, :], onesf[:, :], rsl(cc, n0), start=(cc == 0), stop=(cc == CC - 1)),
                        deps=B("R").deps_read(rslot(cc, n0)) + B("onesf").deps_read() +
                             (B("ps6").deps_write() if cc == 0 else []),
                        tag="sumx")
                    B("R").read(t_sa, rslot(cc, n0))
                    B("onesf").read(t_sa)
                    ta = t_sa
                for cc in range(CC):
                    pr = cc % 2
                    if sq_eng == "act":
                        sqf = lambda e, cc=cc, n0=n0, pr=pr: e.activation(
                            sqt[:, pr * 512:(pr + 1) * 512], rsl(cc, n0), AF.Square)
                    else:
                        sqf = lambda e, cc=cc, n0=n0, pr=pr: e.tensor_mul(
                            sqt[:, pr * 512:(pr + 1) * 512], rsl(cc, n0), rsl(cc, n0))
                    t_sq = p.emit(sq_eng, sqf,
                        deps=B("R").deps_read(rslot(cc, n0)) + B("sqt").deps_write(pr), tag="sq")
                    B("sqt").wrote(t_sq, pr)
                    B("R").read(t_sq, rslot(cc, n0))
                    t_sb2 = p.emit("pe", lambda e, cc=cc, pr=pr: e.matmul(
                        ps[7][0:1, :], onesc[:, :], sqt[:, pr * 512:(pr + 1) * 512],
                        start=(cc == 0), stop=(cc == CC - 1)),
                        deps=[t_sq] + B("onesc").deps_read() + (B("ps7").deps_write() if cc == 0 else []),
                        tag="sumx2")
                    B("sqt").read(t_sb2, pr)
                    B("onesc").read(t_sb2)
                    tb = t_sb2
                B("ps6").wrote(ta)
                B("ps7").wrote(tb)
                t_ea = p.emit("act", lambda e: e.mul(srow_a[:, :], ps[6][0:1, :], 1.0 / DM),
                              deps=[ta] + B("srow_a").deps_write(), tag="mu")
                B("srow_a").wrote(t_ea)
                B("ps6").read(t_ea)
                t_eb = p.emit("act", lambda e: e.mul(srow_b[:, :], ps[7][0:1, :], 1.0 / DM),
                              deps=[tb] + B("srow_b").deps_write(), tag="ex2")
                B("srow_b").wrote(t_eb)
                B("ps7").read(t_eb)
                t_c = p.emit("dve", lambda e: e.tensor_mul(srow_c[:, :], srow_a[:, :], srow_a[:, :]),
                             deps=[t_ea] + B("srow_c").deps_write(), tag="mu2")
                B("srow_c").wrote(t_c)
                B("srow_a").read(t_c)
                t_v = p.emit("dve", lambda e: e.tensor_sub(srow_b[:, :], srow_b[:, :], srow_c[:, :]),
                             deps=[t_eb, t_c], tag="var")
                B("srow_b").wrote(t_v)
                B("srow_c").read(t_v)
                eps = 1e-30 if gamma_fused else LN_EPS
                t_ve = p.emit("dve", lambda e, eps=eps: e.tensor_scalar_add(
                    srow_b[:, :], srow_b[:, :], float(eps)), deps=[t_v], tag="veps")
                B("srow_b").wrote(t_ve)
                t_ln = p.emit("act", lambda e: e.activation(
                    srow_b[:, :], srow_b[:, :], AF.Ln), deps=[t_ve], tag="lnv")
                t_rsd = p.emit("act", lambda e: e.activation(
                    srow_b[:, :], srow_b[:, :], AF.Exp, scale=-0.5), deps=[t_ln], tag="rstd")
                B("srow_b").wrote(t_rsd)
                t_mr = p.emit("dve", lambda e: e.tensor_mul(srow_c[:, :], srow_a[:, :], srow_b[:, :]),
                              deps=[t_rsd, t_ea] + B("srow_c").deps_write(), tag="murstd")
                B("srow_c").wrote(t_mr)
                B("srow_a").read(t_mr)
                B("srow_b").read(t_mr)
                if bcast_pe:
                    t_b1m = p.emit("pe", lambda e: e.matmul(ps[6][:, :], onesr[:, :], srow_b[:, :],
                                                            start=True, stop=True),
                                   deps=[t_rsd] + B("ps6").deps_write() + B("onesr").deps_read(),
                                   tag="bcp1")
                    B("ps6").wrote(t_b1m)
                    B("srow_b").read(t_b1m)
                    B("onesr").read(t_b1m)
                    t_bc1 = p.emit("act", lambda e: e.copy(rep_a[:, :], ps[6][:, :]),
                                   deps=[t_b1m] + B("rep_a").deps_write(), tag="bcp1e")
                    B("ps6").read(t_bc1)
                    B("rep_a").wrote(t_bc1)
                    t_mr2 = B("srow_c").deps_read()
                    t_b2m = p.emit("pe", lambda e: e.matmul(ps[7][:, :], negr[:, :], srow_c[:, :],
                                                            start=True, stop=True),
                                   deps=[t_mr] + B("ps7").deps_write() + B("negr").deps_read(),
                                   tag="bcp2")
                    B("ps7").wrote(t_b2m)
                    B("srow_c").read(t_b2m)
                    B("negr").read(t_b2m)
                    t_bc2 = p.emit("act", lambda e: e.copy(rep_b[:, :], ps[7][:, :]),
                                   deps=[t_b2m] + B("rep_b").deps_write(), tag="bcp2e")
                    B("ps7").read(t_bc2)
                    B("rep_b").wrote(t_bc2)
                else:
                    t_neg = p.emit("dve", lambda e: e.tensor_scalar_mul(srow_c[:, :], srow_c[:, :], -1.0),
                                   deps=[t_mr], tag="negmur")
                    B("srow_c").wrote(t_neg)
                    t_o1 = p.dma("act", browL[0:1, 0:512], srow_b[:, :], deps=[t_rsd],
                                 group="browL", tag="brL1o")
                    B("srow_b").read(t_o1)
                    t_o2 = p.dma("act", browL[0:1, 512:1024], srow_c[:, :], deps=[t_neg],
                                 group="browL", tag="brL2o")
                    B("srow_c").read(t_o2)
                    t_bc1 = p.dma("act", rep_a[:, :], browL[0:1, 0:512].broadcast_to([128, 512]),
                                  deps=[t_o1] + B("rep_a").deps_write(), group="browL", tag="brL1i")
                    B("rep_a").wrote(t_bc1)
                    t_bc2 = p.dma("act", rep_b[:, :], browL[0:1, 512:1024].broadcast_to([128, 512]),
                                  deps=[t_o2] + B("rep_b").deps_write(), group="browL", tag="brL2i")
                    B("rep_b").wrote(t_bc2)
                ap_eng = "pool" if (sq_eng == "pool" and not gamma_fused) else "dve"
                for cc in range(CC):
                    xslot = rslot(cc, n0)
                    if not gamma_fused:
                        t_p1 = p.emit(ap_eng, lambda e, cc=cc, n0=n0: e.tensor_mul(
                            ln_t[:, :], rsl(cc, n0), rep_a[:, :]),
                            deps=[t_bc1] + B("f32t").deps_write() + B("R").deps_read(rslot(cc, n0)),
                            tag="lnap1")
                        t_p2f = lambda e, cc=cc, n0=n0: e.tensor_add(
                            xsl(cc, n0), rep_b[:, :], ln_t[:, :])
                    else:
                        t_p1 = p.emit(ap_eng, lambda e, cc=cc, n0=n0: e.scalar_tensor_tensor(
                            ln_t[:, :], rsl(cc, n0), gamc[:, cc:cc + 1], rep_a[:, :],
                            ALU.mult, ALU.mult),
                            deps=[t_bc1] + B("f32t").deps_write() + B("R").deps_read(rslot(cc, n0)) +
                                 B("gamc").deps_read(), tag="lnap1g")
                        t_p2f = lambda e, cc=cc, n0=n0: e.scalar_tensor_tensor(
                            xsl(cc, n0), rep_b[:, :], gamc[:, cc:cc + 1], ln_t[:, :],
                            ALU.mult, ALU.add)
                    B("rep_a").read(t_p1)
                    B("R").read(t_p1, rslot(cc, n0))
                    B("f32t").wrote(t_p1)
                    t_p2 = p.emit(ap_eng, t_p2f,
                                  deps=[t_bc2, t_p1] + B("xn").deps_write(xslot), tag="lnap2")
                    B("rep_b").read(t_p2)
                    B("f32t").read(t_p2)
                    B("xn").wrote(t_p2, xslot)

        def lora_prep(part=0):
            xdeps = B("xn").deps_read([rslot(cc, 0) for cc in range(CC)])
            p6b = ps[6][:, :].bitcast(BF16)
            lt_toks = []
            for cc in (() if part == 2 else range(CC)):
                t_tr = p.emit("pe", lambda e, cc=cc: e.transpose(
                    p6b[0:16, 0:128], xn[:, cc * S + 1: cc * S + 17], ident[:, :]),
                    deps=xdeps + B("ps6").deps_write() + B("ident").deps_read(), tag="lt_tr")
                B("ps6").wrote(t_tr)
                B("xn").read(t_tr, rslot(cc, 0))
                B("ident").read(t_tr)
                t_ev = p.emit("act", lambda e, cc=cc: e.copy(
                    lt[:, cc * 128:(cc + 1) * 128], p6b[0:16, 0:128]),
                    deps=[t_tr] + (B("lt").deps_write() if cc == 0 else []), tag="lt_ev")
                B("ps6").read(t_ev)
                if cc == 0:
                    B("lt").wrote(t_ev)
                else:
                    B("lt").accum(t_ev)
                lt_toks.append(t_ev)
            if part == 2:
                lt_toks = list(B("lt").deps_read())
            sel_list = ((o1sc, "o1sc", o1c, "o1c"), (o2sc, "o2sc", o2c, "o2c"))
            if part == 1:
                sel_list = sel_list[:1]
            elif part == 2:
                sel_list = sel_list[1:]
            for sel_t, sel_n, dst_t, dst_n in sel_list:
                mms = []
                for cc in range(CC):
                    t_mm = p.emit("pe", lambda e, cc=cc, sel_t=sel_t: e.matmul(
                        ps[7][0:128, cc * 4:(cc + 1) * 4], lt[:, cc * 128:(cc + 1) * 128],
                        sel_t[:, :], start=(cc == 0), stop=(cc == CC - 1)),
                        deps=lt_toks + B(sel_n).deps_read() +
                             (B("ps7").deps_write() if cc == 0 else []), tag=dst_n + "mm")
                    B("lt").read(t_mm)
                    B(sel_n).read(t_mm)
                    mms.append(t_mm)
                B("ps7").wrote(mms[-1])
                t_ev = p.emit("act", lambda e, dst_t=dst_t: e.copy(
                    dst_t[:, :], ps[7][0:128, 0:32]),
                    deps=[mms[-1]] + B(dst_n).deps_write(), tag=dst_n + "ev")
                B("ps7").read(t_ev)
                B(dst_n).wrote(t_ev)
            for half in (() if part == 2 else range(2)):
                t_mm = p.emit("pe", lambda e, half=half: e.matmul(
                    ps[7][0:4, :], e1s[:, :], lt[:, half * 512:(half + 1) * 512],
                    start=True, stop=True),
                    deps=lt_toks + B("ps7").deps_write() + B("e1s").deps_read(), tag="e1mm")
                B("ps7").wrote(t_mm)
                B("lt").read(t_mm)
                B("e1s").read(t_mm)
                t_ev = p.emit("act", lambda e: e.copy(t1[0:4, 0:512], ps[7][0:4, 0:512]),
                              deps=[t_mm] + B("t1").deps_write(), tag="e1ev")
                B("ps7").read(t_ev)
                B("t1").wrote(t_ev)
                for i in range(4):
                    t_cp = p.dma("act", e1z[i:i + 1, i * DM + half * 512: i * DM + (half + 1) * 512],
                                 t1[i:i + 1, 0:512], deps=[t_ev] + B("e1z").deps_write(),
                                 group="e1z", tag="e1cp")
                    B("t1").read(t_cp)
                    B("e1z").accum(t_cp)
            if part == 1:
                return
            e24_toks = []
            for half in range(2):
                t_mm = p.emit("pe", lambda e, half=half: e.matmul(
                    ps[7][0:4, :], e2s[:, :], lt[:, half * 512:(half + 1) * 512],
                    start=True, stop=True),
                    deps=lt_toks + B("ps7").deps_write() + B("e2s").deps_read(), tag="e2mm")
                B("ps7").wrote(t_mm)
                B("lt").read(t_mm)
                B("e2s").read(t_mm)
                t_ev = p.emit("act", lambda e, half=half: e.copy(
                    e24[:, half * 512:(half + 1) * 512], ps[7][0:4, 0:512]),
                    deps=[t_mm] + (B("u4").deps_write() if half == 0 else []), tag="e2ev")
                B("ps7").read(t_ev)
                if half == 0:
                    B("u4").wrote(t_ev)
                else:
                    B("u4").accum(t_ev)
                e24_toks.append(t_ev)
            t_out = p.dma("act", e2scr, e24[:, :], deps=e24_toks, group="e2b", tag="e2out")
            B("u4").read(t_out)
            ins = {}
            for i in range(4):
                for bb in range(2):
                    oc = 2 * i + bb
                    t_in = p.dma("act", e2st[:, oc * 4:(oc + 1) * 4],
                                 e2scr[i:i + 1, bb * 512:(bb + 1) * 512].rearrange(
                                     "o (p j) -> o p j", p=128)[0],
                                 deps=[t_out] + B("e2st").deps_write(), group=f"e2b{oc % 2}",
                                 tag="e2in")
                    B("e2st").wrote(t_in)
                    ins[oc] = t_in
            for g in range(2):
                trs = []
                for ocl in range(4):
                    oc = g * 4 + ocl
                    t_tr = p.emit("pe", lambda e, oc=oc, ocl=ocl: e.transpose(
                        p6b[0:4, ocl * 128:(ocl + 1) * 128], e2st[:, oc * 4:(oc + 1) * 4],
                        ident[:, :]),
                        deps=[ins[oc]] + B("ident").deps_read() +
                             (B("ps6").deps_write() if ocl == 0 else []), tag="e2tr")
                    B("e2st").read(t_tr)
                    trs.append(t_tr)
                B("ps6").wrote(trs[-1])
                t_ev = p.emit("act", lambda e, g=g: e.copy(
                    e2d[:, g * 512:(g + 1) * 512], p6b[0:4, 0:512]),
                    deps=trs + B("e2d").deps_write(), tag="e2dev")
                B("ps6").read(t_ev)
                if g == 0:
                    B("e2d").wrote(t_ev)
                else:
                    B("e2d").accum(t_ev)

        def t1_compute(t0):
            for nh in range(2):
                n0 = t0 + nh * 512
                last = None
                for cc in range(CC):
                    t_mm = p.emit("pe", lambda e, cc=cc, n0=n0: e.matmul(
                        ps[7][0:4, :], o1c[:, cc * 4:(cc + 1) * 4], xsl(cc, n0),
                        start=(cc == 0), stop=(cc == CC - 1)),
                        deps=B("xn").deps_read(rslot(cc, n0)) + B("o1c").deps_read() +
                             (B("ps7").deps_write() if cc == 0 else []), tag="t1mm")
                    B("xn").read(t_mm, rslot(cc, n0))
                    B("o1c").read(t_mm)
                    last = t_mm
                B("ps7").wrote(last)
                t_ev = p.emit("act", lambda e, nh=nh: e.copy(
                    t1[:, nh * 512:(nh + 1) * 512], ps[7][0:4, 0:512]),
                    deps=[last] + (B("t1").deps_write() if nh == 0 else []), tag="t1ev")
                B("ps7").read(t_ev)
                if nh == 0:
                    B("t1").wrote(t_ev)
                else:
                    B("t1").accum(t_ev)

        def ffn(store_seq=None, next_seq=None, hook=None, ln_pre=(), lora_pre=False, t1_pre=(), lora2_first_tile=False, prologue_pe_bcast=False):
            for ti in range(2):
                t0 = ti * NT
                if ti not in ln_pre:
                    layer_norm(t0, NT, False, bcast_pe=prologue_pe_bcast)
                if ti == 0 and prologue_pe_bcast:
                    layer_norm(NT, NT, False, bcast_pe=True)
                if ti == 0 and not lora_pre:
                    lora_prep(part=1 if lora2_first_tile else 0)
                if ti not in t1_pre:
                    t1_compute(t0)
                for g in range(32):
                    wt, wbn = next_w()
                    t_w = wslab("w1", g, wt, wbn)
                    for hcl in range(1):
                        hc = g
                        for nh in range(2):
                            n0 = t0 + nh * 512
                            bi4 = (hc % 2) * 2 + nh
                            bank, bn = ps[bi4], f"ps{bi4}"
                            for cc in range(CC):
                                t_mm = p.emit("pe", lambda e, cc=cc, n0=n0, wt=wt, bank=bank: e.matmul(
                                    bank[:, :], wt[:, cc * 128:(cc + 1) * 128],
                                    xsl(cc, n0), start=(cc == 0), stop=False),
                                    deps=[t_w] + B("xn").deps_read(rslot(cc, n0)) +
                                         (B(bn).deps_write() if cc == 0 else []), tag=f"h{hc}")
                                B(wbn).read(t_mm)
                                B("xn").read(t_mm, rslot(cc, n0))
                                if cc == 0:
                                    B(bn).wrote(t_mm)
                                else:
                                    B(bn).accum(t_mm)
                            t_c1 = p.emit("pe", lambda e, hc=hc, nh=nh, bank=bank: e.matmul(
                                bank[:, :], e1z[:, hc * 128:(hc + 1) * 128],
                                t1[:, nh * 512:(nh + 1) * 512], start=False, stop=True),
                                deps=B("e1z").deps_read() + B("t1").deps_read(), tag=f"c1_{hc}")
                            B("e1z").read(t_c1)
                            B("t1").read(t_c1)
                            B(bn).accum(t_c1)
                            t_g = p.emit("act", lambda e, hc=hc, nh=nh, bank=bank: e.activation(
                                h_act(hc, nh), bank[:, :], AF.Gelu, bias=b1c[:, hc:hc + 1]),
                                deps=[t_c1] + B("arena").deps_write(h_slot(hc)) + B("b1c").deps_read(),
                                tag=f"gelu{hc}")
                            B(bn).read(t_g)
                            B("b1c").read(t_g)
                            if nh == 0:
                                B("arena").wrote(t_g, h_slot(hc))
                            else:
                                B("arena").accum(t_g, h_slot(hc))
                if ti == 0 and lora2_first_tile:
                    lora_prep(part=2)
                for nh in range(2):
                    uev = []
                    for j in range(4):
                        bank, bn = ps[2 + j], f"ps{2 + j}"
                        last = None
                        for qc in range(CC):
                            hc = 8 * j + qc
                            t_mm = p.emit("pe", lambda e, hc=hc, qc=qc, nh=nh, bank=bank: e.matmul(
                                bank[0:4, :], o2c[:, qc * 4:(qc + 1) * 4], h_act(hc, nh),
                                start=(qc == 0), stop=(qc == CC - 1)),
                                deps=B("arena").deps_read(h_slot(hc)) + B("o2c").deps_read() +
                                     (B(bn).deps_write() if qc == 0 else []), tag=f"u{j}")
                            B("o2c").read(t_mm)
                            B("arena").read(t_mm, h_slot(hc))
                            last = t_mm
                        B(bn).wrote(last)
                        t_ev = p.emit("act", lambda e, j=j, bank=bank: e.copy(
                            u4[:, j * 512:(j + 1) * 512], bank[0:4, 0:512]),
                            deps=[last] + (B("u4").deps_write() if j == 0 else []), tag="uev")
                        B(bn).read(t_ev)
                        if j == 0:
                            B("u4").wrote(t_ev)
                        else:
                            B("u4").accum(t_ev)
                        uev.append(t_ev)
                    t_uo = p.dma("act", uscr, u4[:, :], deps=uev, group="ub", tag="u_out")
                    B("u4").read(t_uo)
                    t_uin = p.dma("act", u4[:, :].rearrange("j (i s) -> j i s", i=4),
                                  uscr.rearrange("i (j s) -> j i s", j=4),
                                  deps=[t_uo], group="ub", tag="u_in")
                    B("u4").wrote(t_uin)
                    for g in range(32):
                        wt, wbn = next_w()
                        t_w2l = wslab("w2", g, wt, wbn)
                        for oc in range(CC):
                            bank, bn = ps[oc], f"ps{oc}"
                            for hcl in range(1):
                                hc = g
                                first = (hc == 0)
                                t_mm = p.emit("pe", lambda e, oc=oc, hc=hc, nh=nh, wt=wt, bank=bank: e.matmul(
                                    bank[:, :], wt[:, oc * 128:(oc + 1) * 128],
                                    h_act(hc, nh), start=(hc == 0), stop=False),
                                    deps=[t_w2l] + B("arena").deps_read(h_slot(hc)) +
                                         (B(bn).deps_write() if first else []), tag=f"o{oc}")
                                B(wbn).read(t_mm)
                                B("arena").read(t_mm, h_slot(hc))
                                if first:
                                    B(bn).wrote(t_mm)
                                else:
                                    B(bn).accum(t_mm)
                    for oc in (6, 7, 0, 1, 2, 3, 4, 5):
                        bank, bn = ps[oc], f"ps{oc}"
                        i = oc // 2
                        t_c2 = p.emit("pe", lambda e, oc=oc, i=i, bank=bank: e.matmul(
                            bank[:, :], e2d[:, oc * 128:(oc + 1) * 128], u4[:, i * 512:(i + 1) * 512],
                            start=False, stop=True),
                            deps=[t_uin] + B("e2d").deps_read(), tag=f"c2_{oc}")
                        B("e2d").read(t_c2)
                        B("u4").read(t_c2)
                        B(bn).accum(t_c2)
                        t_t = p.emit("act", lambda e, oc=oc, bank=bank: e.add(
                            tmp_o[:, :], bank[:, :], b2c[:, oc:oc + 1]),
                            deps=[t_c2] + B("f32t").deps_write() + B("b2c").deps_read(),
                            tag=f"badd{oc}")
                        B(bn).read(t_t)
                        B("b2c").read(t_t)
                        B("f32t").wrote(t_t)
                        n0 = t0 + nh * 512
                        t_r = p.emit("dve", lambda e, oc=oc, n0=n0: e.scalar_tensor_tensor(
                            rsl(oc, n0), tmp_o[:, :], 0.5, rsl(oc, n0), ALU.mult, ALU.add),
                            deps=[t_t] + B("R").deps_write(rslot(oc, n0)), tag=f"radd{oc}")
                        B("f32t").read(t_r)
                        B("R").wrote(t_r, rslot(oc, n0))
                        if store_seq is not None:
                            t_st = p.dma("pool", outT[store_seq][oc][:, n0:n0 + 512],
                                         rsl(oc, n0), deps=[t_r], group=f"io{oc % 4}",
                                         tag=f"stR{store_seq}_{oc}_{n0}")
                            B("R").read(t_st, rslot(oc, n0))
                            st_toks.append(t_st)
                            if next_seq is not None:
                                t_ld = p.dma("pool", rsl(oc, n0), xT[next_seq][oc][:, n0:n0 + 512],
                                             deps=[t_st] + B("R").deps_write(rslot(oc, n0)),
                                             group=f"io{oc % 4}", tag=f"ldR{next_seq}_{oc}_{n0}")
                                B("R").wrote(t_ld, rslot(oc, n0))
                    if hook is not None:
                        hook(ti, nh)

        def attn(ln_pre_blocks=0, hook=None):
            for n0 in range(ln_pre_blocks * 512, S, 512):
                ln_block(n0, True)
            for hh in range(NHALF):
                h0 = hh * HPH
                def mhat_head(h):
                    for which, reg0, coloff in (("q", QT0, 0), ("k", KT0, 16)):
                        for nb in range(4):
                            col0 = reg0 + h * S + nb * 512
                            pr = nb % 2
                            t_sq = p.emit("pool", lambda e, col0=col0, pr=pr: e.tensor_mul(
                                sqt[:, pr * 512:(pr + 1) * 512], arena[:, col0:col0 + 512],
                                arena[:, col0:col0 + 512]),
                                deps=B("arena").deps_read(aslot(col0, 512)) + B("sqt").deps_write(pr),
                                tag="nsq")
                            B("sqt").wrote(t_sq, pr)
                            B("arena").read(t_sq, aslot(col0, 512))
                            t_mm = p.emit("pe", lambda e, pr=pr: e.matmul(
                                ps[6][0:1, :], onesc[:, :], sqt[:, pr * 512:(pr + 1) * 512],
                                start=True, stop=True),
                                deps=[t_sq] + B("ps6").deps_write(), tag="nmm")
                            B("ps6").wrote(t_mm)
                            B("sqt").read(t_mm, pr)
                            ncol = coloff + h * 4 + nb
                            t_rm = p.emit("dve", lambda e, ncol=ncol: e.reduce_max(
                                nrm[0:1, ncol:ncol + 1], ps[6][0:1, :], AX.X),
                                deps=[t_mm] + B("nrm").deps_write(), tag="nmax")
                            B("ps6").read(t_rm)
                            B("nrm").accum(t_rm)
                    t_q = p.emit("dve", lambda e, h=h: e.reduce_max(
                        nrm[0:1, 32 + h:33 + h], nrm[0:1, h * 4:(h + 1) * 4], AX.X),
                        deps=B("nrm").deps_read(), tag="nq")
                    B("nrm").accum(t_q)
                    t_k = p.emit("dve", lambda e, h=h: e.reduce_max(
                        nrm[0:1, 40 + h:41 + h], nrm[0:1, 16 + h * 4: 16 + (h + 1) * 4], AX.X),
                        deps=B("nrm").deps_read(), tag="nk")
                    B("nrm").accum(t_k)
                    t_pr = p.emit("dve", lambda e, h=h: e.tensor_mul(
                        nrm[0:1, 48 + h:49 + h], nrm[0:1, 32 + h:33 + h], nrm[0:1, 40 + h:41 + h]),
                        deps=[t_q, t_k], tag="nprod")
                    B("nrm").accum(t_pr)
                    t_sr = p.emit("act", lambda e, h=h: e.activation(
                        nrm[0:1, 48 + h:49 + h], nrm[0:1, 48 + h:49 + h], AF.Sqrt),
                        deps=[t_pr], tag="nsqrt")
                    B("nrm").accum(t_sr)
                    t_nm = p.emit("dve", lambda e, h=h: e.tensor_scalar(
                        nrm[0:1, 56 + h:57 + h], nrm[0:1, 48 + h:49 + h], -1.01, float(CUSH),
                        ALU.mult, ALU.add),
                        deps=[t_sr], tag="negm")
                    B("nrm").accum(t_nm)
                    t_no = p.dma("act", browN[0:1, h:h + 1], nrm[0:1, 56 + h:57 + h],
                                 deps=[t_nm], group="browN", tag="brNo")
                    B("nrm").read(t_no)
                    t_ev = p.dma("act", negm[:, h:h + 1],
                                 browN[0:1, h:h + 1].broadcast_to([128, 1]),
                                 deps=[t_no] + B("negm").deps_write(h), group="browN", tag="brNi")
                    B("negm").wrote(t_ev, h)
                def qkv_slab(kind, hh, col0):
                    gate = prep_done["wq"][1]
                    f0 = {"wq": 0, "wk": 1, "wv": 2}[kind] * DM + hh * 512
                    src = wqs[:, f0:f0 + 512].rearrange("(cc p) f -> p cc f", p=128)
                    dst = arena[:, col0:col0 + 4096].rearrange("p (cc f) -> p cc f", cc=CC)
                    sl = aslot(col0, 4096)
                    t = p.dma("sp", dst, src, deps=B("arena").deps_write(sl) + [gate],
                              group="wA", tag=f"{kind}{hh}")
                    B("arena").wrote(t, sl)
                    return t, sl

                t_wv, wv_sl = qkv_slab("wv", hh, QT0)
                for tt in range(16):
                    bank, bn = ps[tt % 2], f"ps{tt % 2}"
                    last = None
                    for cc in range(CC):
                        t_mm = p.emit("pe", lambda e, cc=cc, tt=tt, bank=bank: e.matmul(
                            bank[:, :], xn[:, cc * S + tt * 128: cc * S + (tt + 1) * 128],
                            arena[:, QT0 + cc * 512: QT0 + (cc + 1) * 512],
                            start=(cc == 0), stop=(cc == CC - 1)),
                            deps=[t_wv] + B("xn").deps_read(rslot(cc, tt * 128)) +
                                 (B(bn).deps_write() if cc == 0 else []), tag="wvmm")
                        B("arena").read(t_mm, wv_sl)
                        B("xn").read(t_mm, rslot(cc, tt * 128))
                        if cc == 0:
                            B(bn).wrote(t_mm)
                        else:
                            B(bn).accum(t_mm)
                        last = t_mm
                    col0 = V0 + tt * 512
                    slots = aslot(col0, 512)
                    t_ev = p.emit("dve", lambda e, col0=col0, bank=bank: e.tensor_copy(
                        arena[:, col0:col0 + 512], bank[:, :]),
                        deps=[last] + B("arena").deps_write(slots), tag="wvev")
                    B(bn).read(t_ev)
                    B("arena").wrote(t_ev, slots)
                def qk_head(h):
                    for kind, reg0, scale_ev in (("wq", QT0, SCALE_ATTN),
                                                 ("wk", KT0, None)):
                        wt, wbn = next_w()
                        wtok = wslab(kind, hh, wt, wbn, part=h)
                        for nb in range(4):
                            bank, bn = ps[nb % 2], f"ps{nb % 2}"
                            last = None
                            for cc in range(CC):
                                t_mm = p.emit("pe", lambda e, cc=cc, nb=nb, wt=wt, bank=bank: e.matmul(
                                    bank[:, :], wt[:, cc * 128:(cc + 1) * 128],
                                    xsl(cc, nb * 512), start=(cc == 0), stop=(cc == CC - 1)),
                                    deps=[wtok] + B("xn").deps_read(rslot(cc, nb * 512)) +
                                         (B(bn).deps_write() if cc == 0 else []), tag=kind + "mm")
                                B(wbn).read(t_mm)
                                B("xn").read(t_mm, rslot(cc, nb * 512))
                                if cc == 0:
                                    B(bn).wrote(t_mm)
                                else:
                                    B(bn).accum(t_mm)
                                last = t_mm
                            col0 = reg0 + h * S + nb * 512
                            slots = aslot(col0, 512)
                            if scale_ev is None:
                                t_ev = p.emit("act", lambda e, col0=col0, bank=bank: e.copy(
                                    arena[:, col0:col0 + 512], bank[:, :]),
                                    deps=[last] + B("arena").deps_write(slots), tag=kind + "ev")
                            else:
                                t_ev = p.emit("act", lambda e, col0=col0, bank=bank, scale_ev=scale_ev: e.mul(
                                    arena[:, col0:col0 + 512], bank[:, :], float(scale_ev)),
                                    deps=[last] + B("arena").deps_write(slots), tag=kind + "ev")
                            B(bn).read(t_ev)
                            B("arena").wrote(t_ev, slots)
                units = [(h, qb) for h in range(HPH) for qb in range(4)]
                gkt = [0]

                def emit_S_exp(h, qb, kt):
                    sb_i = (0, 1, 7)[gkt[0] % 3]
                    bank, bn = ps[sb_i], f"ps{sb_i}"
                    pslot = gkt[0] % 4
                    gkt[0] += 1
                    diag = (kt // 4 == qb)
                    dep0 = B(bn).deps_write()
                    if diag:
                        m = kt - 4 * qb
                        t_pl = p.emit("pe", lambda e, m=m, bank=bank: e.matmul(
                            bank[:, :], ident[:, :], maskm[:, 384 - 128 * m: 896 - 128 * m],
                            start=True, stop=False),
                            deps=dep0 + B("maskm").deps_read() + B("ident").deps_read(),
                            tag="spl")
                        B(bn).wrote(t_pl)
                        B("maskm").read(t_pl)
                        B("ident").read(t_pl)
                        dep0 = []
                    kcol = KT0 + h * S + kt * 128
                    qcol = QT0 + h * S + qb * 512
                    t_s = p.emit("pe", lambda e, kcol=kcol, qcol=qcol, bank=bank, diag=diag: e.matmul(
                        bank[:, :], arena[:, kcol:kcol + 128], arena[:, qcol:qcol + 512],
                        start=not diag, stop=True),
                        deps=dep0 + B("arena").deps_read(aslot(kcol, 128) + aslot(qcol, 512)),
                        tag="smm")
                    if diag:
                        B(bn).accum(t_s)
                    else:
                        B(bn).wrote(t_s)
                    B("arena").read(t_s, aslot(kcol, 128) + aslot(qcol, 512))
                    t_e = p.emit("act", lambda e, pslot=pslot, h=h, bank=bank: e.activation(
                        pb[:, pslot * 512:(pslot + 1) * 512], bank[:, :], AF.Exp,
                        bias=negm[:, h:h + 1]),
                        deps=[t_s] + B("pb").deps_write(pslot) + B("negm").deps_read(h),
                        tag="exp")
                    B(bn).read(t_e)
                    B("negm").read(t_e, h)
                    B("pb").wrote(t_e, pslot)
                    return t_e, pslot

                def emit_rspv(h, qb, kt, t_e, pslot, first, last, ubank):
                    rsb, rsn = (ps[3], "ps3") if ubank == 0 else (ps[6], "ps6")
                    ob, obn = (ps[2], "ps2") if ubank == 0 else (ps[5], "ps5")
                    t_rs = p.emit("pe", lambda e, pslot=pslot, first=first, last=last, rsb=rsb: e.matmul(
                        rsb[0:1, :], onesc[:, :], pb[:, pslot * 512:(pslot + 1) * 512],
                        start=first, stop=last),
                        deps=[t_e] + (B(rsn).deps_write() if first else []), tag="rsmm")
                    B("pb").read(t_rs, pslot)
                    if first:
                        B(rsn).wrote(t_rs)
                    else:
                        B(rsn).accum(t_rs)
                    vcol = V0 + kt * 512 + h * 128
                    t_pv = p.emit("pe", lambda e, vcol=vcol, pslot=pslot, first=first, last=last, ob=ob: e.matmul(
                        ob[:, :], arena[:, vcol:vcol + 128], pb[:, pslot * 512:(pslot + 1) * 512],
                        start=first, stop=last),
                        deps=[t_e] + B("arena").deps_read(aslot(vcol, 128)) +
                             (B(obn).deps_write() if first else []), tag="pvmm")
                    B("pb").read(t_pv, pslot)
                    B("arena").read(t_pv, aslot(vcol, 128))
                    if first:
                        B(obn).wrote(t_pv)
                    else:
                        B(obn).accum(t_pv)

                def emit_tail(h, qb, ubank):
                    rsb, rsn = (ps[3], "ps3") if ubank == 0 else (ps[6], "ps6")
                    ob, obn = (ps[2], "ps2") if ubank == 0 else (ps[5], "ps5")
                    t_l = p.emit("act", lambda e, rsb=rsb: e.activation(recr[:, :], rsb[0:1, :], AF.Ln),
                                 deps=B(rsn).deps_read() + B("srow_a").deps_write(), tag="rln")
                    B(rsn).read(t_l)
                    B("srow_a").wrote(t_l)
                    t_x = p.emit("act", lambda e: e.activation(recr[:, :], recr[:, :], AF.Exp, scale=-1.0),
                                 deps=[t_l], tag="rexp")
                    B("srow_a").wrote(t_x)
                    t_ro = p.dma("act", browF, recr[:, :], deps=[t_x], group="browF", tag="brFo")
                    B("srow_a").read(t_ro)
                    t_rb2 = p.dma("act", f32t[:, :], browF[0:1, :].broadcast_to([128, 512]),
                                  deps=[t_ro] + B("f32t").deps_write(), group="browF", tag="brFi")
                    B("f32t").wrote(t_rb2)
                    ocol = OA0 + h * S + qb * 512
                    t_oa = p.emit("dve", lambda e, ocol=ocol, ob=ob: e.tensor_mul(
                        arena[:, ocol:ocol + 512], ob[:, :], f32t[:, :]),
                        deps=[t_rb2] + B(obn).deps_read() + B("arena").deps_write(aslot(ocol, 512)),
                        tag="oaev")
                    B(obn).read(t_oa)
                    B("f32t").read(t_oa)
                    B("arena").wrote(t_oa, aslot(ocol, 512))

                pending = []
                pre = {}
                DEPTH = 2
                fstate = {"i": 0}
                qk_done = set()

                def emit_flash_unit():
                    ui = fstate["i"]
                    fstate["i"] += 1
                    h, qb = units[ui]
                    nkt = qb * 4 + 4
                    ubank = ui % 2

                    def push(item):
                        pending.append(item)
                        if len(pending) > DEPTH:
                            emit_rspv(*pending.pop(0))

                    done = pre.pop((h, qb), [])
                    for kt, t_e, pslot in done:
                        push((h, qb, kt, t_e, pslot, kt == 0, kt == nkt - 1, ubank))
                    for kt in range(len(done), nkt):
                        t_e, pslot = emit_S_exp(h, qb, kt)
                        push((h, qb, kt, t_e, pslot, kt == 0, kt == nkt - 1, ubank))
                    if ui + 1 < len(units) and units[ui + 1][0] in qk_done:
                        nh_, nqb_ = units[ui + 1]
                        lst = []
                        for kt2 in range(min(2, nqb_ * 4 + 4)):
                            te2, psl2 = emit_S_exp(nh_, nqb_, kt2)
                            lst.append((kt2, te2, psl2))
                        pre[(nh_, nqb_)] = lst
                    while pending:
                        emit_rspv(*pending.pop(0))
                    emit_tail(h, qb, ubank)

                for h in range(HPH):
                    qk_head(h)
                    qk_done.add(h)
                    mhat_head(h)
                    if h >= 1:
                        for _ in range(4):
                            emit_flash_unit()
                for _ in range(4):
                    emit_flash_unit()
                gate = prep_done["wo"][hh]
                src = wos[hh * 512:(hh + 1) * 512, :].rearrange("(fc p) c -> p fc c", p=128)
                wo_sl = aslot(KT0, 4096)
                dst = arena[:, KT0:KT0 + 4096].rearrange("p (fc c) -> p fc c", fc=4)
                t_wo = p.dma("sp", dst, src, deps=B("arena").deps_write(wo_sl) + [gate],
                             group="wA", tag=f"wo{hh}")
                B("arena").wrote(t_wo, wo_sl)
                for qb in range(4):
                    for oc in range(CC):
                        bank, bn = ps[5 + (oc % 3)], f"ps{5 + (oc % 3)}"
                        last = None
                        for fc in range(HPH):
                            ocol = OA0 + fc * S + qb * 512
                            t_mm = p.emit("pe", lambda e, fc=fc, oc=oc, ocol=ocol, bank=bank: e.matmul(
                                bank[:, :], arena[:, KT0 + fc * 1024 + oc * 128: KT0 + fc * 1024 + (oc + 1) * 128],
                                arena[:, ocol:ocol + 512], start=(fc == 0), stop=(fc == HPH - 1)),
                                deps=[t_wo] + B("arena").deps_read(aslot(ocol, 512)) +
                                     (B(bn).deps_write() if fc == 0 else []), tag="opmm")
                            B("arena").read(t_mm, wo_sl)
                            B("arena").read(t_mm, aslot(ocol, 512))
                            if fc == 0:
                                B(bn).wrote(t_mm)
                            else:
                                B(bn).accum(t_mm)
                            last = t_mm
                        n0 = qb * 512
                        t_r = p.emit("dve", lambda e, oc=oc, n0=n0, bank=bank: e.tensor_add(
                            rsl(oc, n0), rsl(oc, n0), bank[:, :]),
                            deps=[last] + B("R").deps_write(rslot(oc, n0)), tag="raddA")
                        B(bn).read(t_r)
                        B("R").wrote(t_r, rslot(oc, n0))
                    if hook is not None:
                        hook(hh, qb)

        def load_R_blocks(seq, b0, b1):
            for cc in range(CC):
                sl = [cc * 4 + b for b in range(b0, b1)]
                t = p.dma("sp", R[:, cc * S + b0 * 512: cc * S + b1 * 512],
                          xT[seq][cc][:, b0 * 512: b1 * 512],
                          deps=B("R").deps_write(sl), group=f"ioA{cc % 4}",
                          tag=f"ldR{seq}_{cc}_{b0}")
                B("R").wrote(t, sl)

        def store_R(seq):
            dst = outT[seq].rearrange("cc p s -> p cc s")
            src = R[:, :].rearrange("p (cc s) -> p cc s", cc=CC)
            t = p.dma("pool", dst, src, deps=B("R").deps_read() + B("R").deps_write(),
                      group="io0", tag=f"stR{seq}")
            B("R").read(t)
            return t

        st_toks = []
        _blocks = do_prep()
        load_R_blocks(0, 0, 1)
        load_R_blocks(0, 1, 2)
        load_R_blocks(0, 2, 4)
        emit_prep([b for b in _blocks if b[0] == "w1"])
        emit_prep([b for b in _blocks if b[0] != "w1"])
        for seq in range(BPC):
            if n_sub >= 3:
                first = (seq == 0)

                def ffn1_hook(ti, nh):
                    if ti == 0 and nh == 0:
                        if not first:
                            ln_block(NT, False, sq_eng='pool')
                            ln_block(NT + 512, False, sq_eng='pool')
                        t1_compute(NT)
                    elif ti == 1 and nh == 0:
                        ln_block(0, True, sq_eng='pool')
                        ln_block(512, True, sq_eng='pool')
                        ln_block(NT, True, sq_eng='pool')
                    elif ti == 1 and nh == 1:
                        ln_block(NT + 512, True, sq_eng='pool')

                ffn(hook=ffn1_hook, ln_pre=(1,) if first else (0, 1),
                    lora_pre=not first, t1_pre=(1,) if first else (0, 1),
                    lora2_first_tile=True, prologue_pe_bcast=first)

                def attn_hook(hh, qb):
                    if hh != NHALF - 1:
                        return
                    if qb == 0:
                        ln_block(0, False, sq_eng='pool')
                    elif qb == 1:
                        ln_block(512, False, sq_eng='pool')
                    elif qb == 2:
                        lora_prep(part=1)
                        ln_block(NT, False, sq_eng='pool')
                    elif qb == 3:
                        t1_compute(0)
                        ln_block(NT + 512, False, sq_eng='pool')

                nxt = seq + 1 if seq + 1 < BPC else None

                def ffn2_hook(ti, nh):
                    if ti == 0 and nh == 0:
                        t1_compute(NT)
                    if nxt is not None:
                        if ti == 0 and nh == 1:
                            ln_block(0, False, sq_eng='pool')
                        elif ti == 1 and nh == 0:
                            ln_block(512, False, sq_eng='pool')
                            lora_prep(part=1)
                            t1_compute(0)

                attn(ln_pre_blocks=4, hook=attn_hook)
                ffn(store_seq=seq, next_seq=nxt, hook=ffn2_hook,
                    ln_pre=(0, 1), lora_pre=True, t1_pre=(1,), lora2_first_tile=True)
            else:
                if seq > 0:
                    load_R(seq)
                if n_sub >= 1:
                    ffn()
                if n_sub >= 2:
                    attn()
                st_toks.append(store_R(seq))
        p.emit("pool", lambda e: e.engine_nop(), deps=st_toks, tag="final_wait")
        p.emit("act", lambda e: e.mul(srow_c[0:1, 0:1], srow_c[0:1, 0:1], 1.0),
               deps=B("srow_c").deps_write(), tag="trail_nop_act")
        p.emit("dve", lambda e: e.engine_nop(), deps=[], tag="trail_nop_dve")

        stats = p.finalize_and_emit(ctx)
        stats["sbuf_left"] = sbuf_left
    return nc, stats



def shard_inputs(L0_attn_x, L0_attn_gamma, L0_attn_w_qkv, L0_attn_w_out,
                 L0_ffn_W1, L0_ffn_b1, L0_ffn_W2, L0_ffn_b2, L0_ffn_scale):
    f32 = np.float32
    com = {
        "w1T": np.ascontiguousarray(np.asarray(L0_ffn_W1, f32).T),
        "w2T": np.ascontiguousarray(np.asarray(L0_ffn_W2, f32).T),
        "wqkvT": np.ascontiguousarray(np.asarray(L0_attn_w_qkv, f32).T),
        "woutT": np.ascontiguousarray(np.asarray(L0_attn_w_out, f32).T),
        "b1h": np.ascontiguousarray(np.asarray(L0_ffn_b1, f32).reshape(HC, 128).T),
        "b2h": np.ascontiguousarray(np.asarray(L0_ffn_b2, f32).reshape(CC, 128).T),
        "gamh": np.ascontiguousarray(np.asarray(L0_attn_gamma, f32).reshape(CC, 128).T),
        "scaleh": np.asarray(L0_ffn_scale, f32).reshape(1, 1),
    }
    x = np.asarray(L0_attn_x, f32)
    maps = []
    for core in range(x.shape[0] // BPC):
        xs = x[core * BPC:(core + 1) * BPC]
        xTc = np.ascontiguousarray(xs.transpose(0, 2, 1)).reshape(BPC, CC, 128, S)
        maps.append({**com, "xT": xTc})
    return maps


def gather_outputs(results):
    outs = []
    for r in results:
        o = np.asarray(r["outT"]).reshape(BPC, DM, S).transpose(0, 2, 1)
        outs.append(o)
    return np.ascontiguousarray(np.concatenate(outs, axis=0))



from concourse.bass_utils import run_bass_kernel_spmd


def kernel(**inputs):
    maps = shard_inputs(**inputs)
    nc, _stats = build(3)
    res = run_bass_kernel_spmd(nc, maps, core_ids=list(range(8)))
    return gather_outputs(res.results)


# version 26
# speedup vs baseline: 1.0399x; 1.0204x over previous
import contextlib
import numpy as np
import ml_dtypes
import concourse.bass as bass
import concourse.mybir as mybir

F32 = mybir.dt.float32
BF16 = mybir.dt.bfloat16
AF = mybir.ActivationFunctionType
ALU = mybir.AluOpType
AX = mybir.AxisListType
bfnp = ml_dtypes.bfloat16

ENGINES = ("pe", "act", "dve", "pool", "sp")


class Tok:
    __slots__ = ("eng", "idx", "sem", "val")

    def __init__(self, eng, idx):
        self.eng, self.idx, self.sem, self.val = eng, idx, None, None


class Op:
    __slots__ = ("fn", "deps", "tok", "dma_group", "needs_inc", "tag", "gseq")

    def __init__(self, fn, deps, tok, dma_group, tag, gseq):
        self.fn, self.deps, self.tok = fn, deps, tok
        self.dma_group, self.needs_inc, self.tag = dma_group, False, tag
        self.gseq = gseq


class Prog:
    def __init__(self, nc):
        self.nc = nc
        self.ops = {e: [] for e in ENGINES}
        self.n_emitted = 0
        self.last_dma = {}
        self.group_mode = {}

    def emit(self, eng, fn, deps=(), dma_group=None, tag=""):
        dl = []
        seen = {}
        for d in deps:
            if d is None:
                continue
            m = False
            if isinstance(d, tuple):
                m = True
                d = d[1]
            k = (d.eng, d.idx)
            if k in seen:
                if m and not seen[k][1]:
                    seen[k][0][:] = [d, True]
                continue
            ent = [d, m]
            seen[k] = (ent, m)
            dl.append(ent)
        tok = Tok(eng, len(self.ops[eng]))
        self.ops[eng].append(Op(fn, dl, tok, dma_group, tag, self.n_emitted))
        self.n_emitted += 1
        return tok

    def dma(self, eng, out_ap, in_ap, deps=(), group=None, serial=True, tag="", **kw):
        assert group is not None
        mode = (eng, "serial" if serial else "batch")
        assert self.group_mode.setdefault(group, mode) == mode, (group, mode)

        def fn(e):
            return e.dma_start(out_ap, in_ap, **kw)

        deps = list(deps)
        if serial and group in self.last_dma:
            deps.append(self.last_dma[group])
        tok = self.emit(eng, fn, deps=deps, dma_group=group, tag=tag)
        self.last_dma[group] = tok
        return tok

    def finalize_and_emit(self, ctx):
        nc = self.nc

        def elided(e, d):
            dop = self.ops[d.eng][d.idx]
            return e == "pe" and d.eng == "pe" and dop.dma_group is None

        def margin_tok(d, consumer):
            lst = self.ops[d.eng]
            j = d.idx + 1
            if j < len(lst) and lst[j].gseq < consumer.gseq:
                return lst[j]
            return None

        for e in ENGINES:
            for op in self.ops[e]:
                if op.dma_group is not None:
                    op.needs_inc = True
                for d, m in op.deps:
                    if elided(e, d):
                        continue
                    self.ops[d.eng][d.idx].needs_inc = True
                    if m:
                        mo = margin_tok(d, op)
                        if mo is not None and mo.dma_group is None:
                            mo.needs_inc = True
        sems = {}

        def get_sem(name):
            if name not in sems:
                sems[name] = ctx.enter_context(nc.semaphore(name=name))
            return sems[name]

        counts = {}
        for e in ENGINES:
            for op in self.ops[e]:
                if not op.needs_inc:
                    continue
                if op.dma_group is not None:
                    sname, amt = "d_" + op.dma_group, 16
                else:
                    sname, amt = "c_" + e, 1
                counts[sname] = counts.get(sname, 0) + amt
                op.tok.sem, op.tok.val = sname, counts[sname]
        stats = {"waits": 0, "incs": 0, "sems": len(counts), "ops": self.n_emitted}

        def make_runner(e):
            oplist = self.ops[e]

            def run(engine_obj):
                hiwater = {}
                for op in oplist:
                    waits = {}
                    for d, m in op.deps:
                        if elided(e, d):
                            continue
                        s, v = d.sem, d.val
                        if m:
                            mo = margin_tok(d, op)
                            if mo is not None and mo.dma_group is None:
                                s, v = mo.tok.sem, mo.tok.val
                        assert s is not None, (e, op.tag, d.eng, d.idx)
                        if hiwater.get(s, 0) >= v:
                            continue
                        if waits.get(s, 0) < v:
                            waits[s] = v
                    for s, v in waits.items():
                        engine_obj.wait_ge(get_sem(s), v)
                        hiwater[s] = v
                        stats["waits"] += 1
                    ins = op.fn(engine_obj)
                    if op.needs_inc:
                        amt = 16 if op.dma_group is not None else 1
                        ins.then_inc(get_sem(op.tok.sem), amt)
                        stats["incs"] += 1

            return run

        with nc.Block() as block:
            block.tensor(make_runner("pe"))
            block.scalar(make_runner("act"))
            block.vector(make_runner("dve"))
            block.gpsimd(make_runner("pool"))
            block.sync(make_runner("sp"))
        return stats


class Buf:

    def __init__(self, nslots=1, margin=False):
        self.n = nslots
        self.margin = margin
        self.w = [[] for _ in range(nslots)]
        self.r = [[] for _ in range(nslots)]

    def _sl(self, s):
        if s is None:
            return range(self.n)
        if isinstance(s, int):
            return (s,)
        return s

    def deps_write(self, s=None):
        out = []
        for i in self._sl(s):
            out += self.w[i]
            if self.margin:
                out += [("m", t) for t in self.r[i]]
            else:
                out += self.r[i]
        return out

    def deps_read(self, s=None):
        out = []
        for i in self._sl(s):
            out += self.w[i]
        return out

    def wrote(self, tok, s=None):
        for i in self._sl(s):
            self.w[i] = [tok]
            self.r[i] = []

    def accum(self, tok, s=None):
        for i in self._sl(s):
            self.w[i].append(tok)

    def read(self, tok, s=None):
        for i in self._sl(s):
            self.r[i].append(tok)


DM = 1024
S = 2048
BPC = 2
H = 4096
CC = 8
HC = 32
HEADS = 8
DH = 128
NHALF = 2
HPH = 4
SCALE_ATTN = DH ** -0.5
CUSH = 30.0
NT = 1024
LN_EPS = 1e-5
PSUM_MARGIN = False


def host_constants():
    c = {}
    c["ones_col_bf"] = np.ones((128, 1), bfnp)
    c["ones_col_f32"] = np.ones((128, 1), np.float32)
    c["ones_row_f32"] = np.ones((1, 128), np.float32)
    c["negones_row_f32"] = -np.ones((1, 128), np.float32)
    c["ident_bf"] = np.eye(128, dtype=bfnp)
    j = np.arange(896)[None, :] - 384.0
    k = np.arange(128)[:, None]
    c["mask_master_bf"] = np.where(j >= k, 0.0, -1e30).astype(bfnp)

    def sel(rows):
        m = np.zeros((16, 4), np.float32)
        for i, r in enumerate(rows):
            m[r, i] = 1.0
        return m.astype(bfnp)

    c["o1sel_bf"] = sel([1, 3, 5, 7])
    c["o2sel_bf"] = sel([9, 11, 13, 15])
    c["e1sel_bf"] = sel([0, 2, 4, 6])
    c["e2sel_bf"] = sel([8, 10, 12, 14])
    return c


def build(n_sub=3):
    nc = bass.Bass("TRN2", target_bir_lowering=False, debug=False)
    C = host_constants()

    xT = nc.dram_tensor("xT", [BPC, CC, 128, S], F32, kind="ExternalInput").ap()
    outT = nc.dram_tensor("outT", [BPC, CC, 128, S], F32, kind="ExternalOutput").ap()
    w1T = nc.dram_tensor("w1T", [DM, H], F32, kind="ExternalInput").ap()
    w2T = nc.dram_tensor("w2T", [H, DM], F32, kind="ExternalInput").ap()
    wqkvT = nc.dram_tensor("wqkvT", [DM, 3 * DM], F32, kind="ExternalInput").ap()
    woutT = nc.dram_tensor("woutT", [DM, DM], F32, kind="ExternalInput").ap()
    b1h = nc.dram_tensor("b1h", [128, HC], F32, kind="ExternalInput").ap()
    b2h = nc.dram_tensor("b2h", [128, CC], F32, kind="ExternalInput").ap()
    gamh = nc.dram_tensor("gamh", [128, CC], F32, kind="ExternalInput").ap()
    scaleh = nc.dram_tensor("scaleh", [1, 1], F32, kind="ExternalInput").ap()

    consts = {k: nc.inline_tensor(v, name=k).ap() for k, v in C.items()}
    e2scr = nc.dram_tensor("e2scr", [4, DM], BF16, kind="Internal").ap()
    w1s = nc.dram_tensor("w1s", [DM, H], BF16, kind="Internal").ap()
    w2s = nc.dram_tensor("w2s", [H, DM], BF16, kind="Internal").ap()
    wqs = nc.dram_tensor("wqs", [DM, 3 * DM], BF16, kind="Internal").ap()
    wos = nc.dram_tensor("wos", [DM, DM], BF16, kind="Internal").ap()
    uscr = nc.dram_tensor("uscr", [4, 4 * 512], BF16, kind="Internal").ap()
    browL = nc.dram_tensor("browL", [1, 1024], F32, kind="Internal").ap()
    browF = nc.dram_tensor("browF", [1, 512], F32, kind="Internal").ap()
    browN = nc.dram_tensor("browN", [1, 8], F32, kind="Internal").ap()

    p = Prog(nc)
    bufs = {}
    ctx = contextlib.ExitStack()
    with ctx:
        def sb(name, shape, dtype, nslots=1):
            t = ctx.enter_context(nc.sbuf_tensor(name, shape, dtype))
            bufs[name] = Buf(nslots)
            return t

        R = sb("R", [128, CC * S], F32, nslots=32)
        arena = sb("arena", [128, HC * NT], BF16, nslots=32)
        xn = sb("xn", [128, CC * S], BF16, nslots=32)
        wb0 = sb("wb0", [128, 1024], BF16)
        wb1 = sb("wb1", [128, 1024], BF16)
        wb2 = sb("wb2", [128, 1024], BF16)
        f32t = sb("f32t", [128, 512], F32)
        ln_t = f32t
        tmp_o = f32t
        recb = f32t
        sqt = sb("sqt", [128, 1024], BF16, nslots=2)
        qsq = sqt
        srow_a = sb("srow_a", [1, 512], F32)
        srow_b = sb("srow_b", [1, 512], F32)
        srow_c = sb("srow_c", [1, 512], F32)
        e1z = sb("e1z", [4, H], BF16)
        t1 = sb("t1", [4, NT], BF16)
        lt = sb("lt", [16, DM], BF16)

        e2d = sb("e2d", [4, CC * 128], BF16)
        u4 = sb("u4", [4, 4 * 512], BF16)
        e24 = u4[:, 0:DM]
        e2st = sb("e2st", [128, 32], BF16)
        pb = sb("pb", [128, 512 * 4], BF16, nslots=4)
        recr = srow_a
        rep_a = sb("rep_a", [128, 512], F32)
        rep_b = sb("rep_b", [128, 512], F32)
        nrm = sb("nrm", [1, 64], F32)
        negm = sb("negm", [128, HPH], F32, nslots=HPH)
        b1c = sb("b1c", [128, HC], F32)
        b2c = sb("b2c", [128, CC], F32)
        gamc = sb("gamc", [128, CC], F32)
        scol = sb("scol", [16, 1], F32)
        onesc = sb("onesc", [128, 1], BF16)
        onesr = sb("onesr", [1, 128], F32)
        negr = sb("negr", [1, 128], F32)
        onesf = sb("onesf", [128, 1], F32)
        ident = sb("ident", [128, 128], BF16)
        maskm = sb("maskm", [128, 896], BF16)
        o1s = sb("o1s", [16, 4], BF16)
        o2s = sb("o2s", [16, 4], BF16)
        e1s = sb("e1s", [16, 4], BF16)
        e2s = sb("e2s", [16, 4], BF16)
        o1sc = sb("o1sc", [16, 4], BF16)
        o2sc = sb("o2sc", [16, 4], BF16)
        o1c = sb("o1c", [128, CC * 4], BF16)
        o2c = sb("o2c", [128, CC * 4], BF16)
        sbuf_left = nc.sbuf_bytes_remaining
        ps = []
        for i in range(8):
            ps.append(ctx.enter_context(nc.psum_tensor(f"ps{i}", [128, 512], F32)))
            bufs[f"ps{i}"] = Buf(margin=PSUM_MARGIN)

        def B(name):
            return bufs[name]

        def rsl(cc, t0_, w=512):
            return R[:, cc * S + t0_: cc * S + t0_ + w]

        def rslot(cc, t0_):
            return cc * 4 + t0_ // 512

        def xsl(cc, t0_, w=512):
            return xn[:, cc * S + t0_: cc * S + t0_ + w]

        def h_act(hc, nh):
            return arena[:, hc * NT + nh * 512: hc * NT + (nh + 1) * 512]

        def h_slot(hc):
            return hc

        KT0, V0, QT0, OA0 = 0, HPH * S, 2 * HPH * S, 3 * HPH * S

        def aslot(col0, width):
            return list(range(col0 // NT, (col0 + width + NT - 1) // NT))

        ld = []
        for dst, src, tag in ((b1c, b1h, "b1"), (b2c, b2h, "b2"), (gamc, gamh, "gam"),
                              (onesc, consts["ones_col_bf"], "oc"),
                              (onesr, consts["ones_row_f32"], "or"),
                              (negr, consts["negones_row_f32"], "nr"),
                              (onesf, consts["ones_col_f32"], "of"),
                              (ident, consts["ident_bf"], "id"),
                              (maskm, consts["mask_master_bf"], "mask"),
                              (o1s, consts["o1sel_bf"], "o1s"),
                              (o2s, consts["o2sel_bf"], "o2s"),
                              (e1s, consts["e1sel_bf"], "e1s"),
                              (e2s, consts["e2sel_bf"], "e2s")):
            t = p.dma("sp", dst[:, :], src, group="ini", serial=False, tag=tag)
            ld.append(t)
        ini_tok = p.emit("dve", lambda e: e.engine_nop(), deps=ld, tag="ini_fence")
        for name in ("b1c", "b2c", "gamc", "onesc", "onesf", "onesr", "negr", "ident",
                     "maskm", "o1s", "o2s", "e1s", "e2s", "srow_a"):
            B(name).wrote(ini_tok)
        t_sc_ev = p.dma("sp", scol[:, :], scaleh[0:1, 0:1].broadcast_to([16, 1]),
                        deps=[ini_tok], group="scb", tag="scbc")
        B("scol").wrote(t_sc_ev)
        for src_s, dst_s, nm in ((o1s, o1sc, "o1sc"), (o2s, o2sc, "o2sc")):
            t = p.emit("dve", lambda e, src_s=src_s, dst_s=dst_s: e.tensor_scalar_mul(
                dst_s[:, :], src_s[:, :], scol[:, 0:1]),
                deps=[t_sc_ev, ini_tok], tag=nm)
            B(nm).wrote(t)
            B("scol").read(t)
        t_ms = p.emit("dve", lambda e: e.memset(e1z[:, :], 0.0), deps=[], tag="e1z_memset")
        B("e1z").wrote(t_ms)

        warm_tok = ini_tok
        for wi in range(96):
            warm_tok = p.emit("pe", lambda e: e.matmul(
                ps[7][0:1, 0:512], onesc[:, :], maskm[:, 0:512],
                start=True, stop=True), deps=[warm_tok], tag=f"warm{wi}")

        prep_done = {}

        def do_prep():
            blocks = []
            for b in range(4):
                src = w1T[:, b * 1024:(b + 1) * 1024].rearrange("(g p) h -> p g h", p=128)
                dst = w1s[:, b * 1024:(b + 1) * 1024].rearrange("(g p) h -> p g h", p=128)
                blocks.append(("w1", b, src, dst, 8192))
            for b in range(4):
                src = w2T[b * 1024:(b + 1) * 1024, :].rearrange("(g p) o -> p g o", p=128)
                dst = w2s[b * 1024:(b + 1) * 1024, :].rearrange("(g p) o -> p g o", p=128)
                blocks.append(("w2", b, src, dst, 8192))
            for b in range(4):
                src = wqkvT[b * 256:(b + 1) * 256, :].rearrange("(g p) f -> p g f", p=128)
                dst = wqs[b * 256:(b + 1) * 256, :].rearrange("(g p) f -> p g f", p=128)
                blocks.append(("wq", b, src, dst, 6144))
            for b in range(2):
                src = woutT[b * 512:(b + 1) * 512, :].rearrange("(g p) c -> p g c", p=128)
                dst = wos[b * 512:(b + 1) * 512, :].rearrange("(g p) c -> p g c", p=128)
                blocks.append(("wo", b, src, dst, 4096))
            for kind in ("w1", "w2", "wq", "wo"):
                prep_done[kind] = {}
            return blocks

        def emit_prep(blocks):
            for bi, (kind, b, src, dst, width) in enumerate(blocks):
                par = bi % 2
                slot0 = par * 8
                stg = arena[:, slot0 * NT: slot0 * NT + width]
                sl = list(range(slot0, slot0 + 8))
                stg3 = stg.rearrange("p (g x) -> p g x", x=src.shape[2])
                t_l = p.dma("pool", stg3, src, deps=B("arena").deps_write(sl),
                            group=f"prep_l{par}", tag=f"pl_{kind}{b}")
                B("arena").wrote(t_l, sl)
                t_s = p.dma("sp", dst, stg3, deps=[t_l], group=f"prep_s{par}",
                            tag=f"ps_{kind}{b}")
                B("arena").read(t_s, sl)
                prep_done[kind][b] = t_s
            prep_done["wk"] = prep_done["wv"] = prep_done["wq"]

        def wslab(kind, arg, wt, wbname, part=0):
            if kind == "w1":
                gate = prep_done["w1"][arg // 8]
                src = w1s[:, arg * 128:(arg + 1) * 128].rearrange("(cc p) h -> p cc h", p=128)
                dst = wt[:, :].rearrange("p (cc h) -> p cc h", cc=CC)
            elif kind == "w2":
                gate = prep_done["w2"][arg // 8]
                src = w2s[arg * 128:(arg + 1) * 128, :].rearrange("(hh p) o -> p hh o", p=128)
                dst = wt[:, :].rearrange("p (hh o) -> p hh o", hh=1)
            elif kind in ("wq", "wk"):
                gate = prep_done["wq"][3]
                f0 = {"wq": 0, "wk": 1}[kind] * DM + arg * 512 + part * 128
                src = wqs[:, f0:f0 + 128].rearrange("(cc p) f -> p cc f", p=128)
                dst = wt[:, :].rearrange("p (cc f) -> p cc f", cc=CC)
            else:
                gate = prep_done["wo"][arg]
                r0 = arg * 512 + part * 128
                src = wos[r0:r0 + 128, :].rearrange("(fc p) c -> p fc c", p=128)
                dst = wt[:, :].rearrange("p (fc c) -> p fc c", fc=1)
            deps = B(wbname).deps_write() + [gate]
            grp = {"wb0": "wA", "wb1": "wB", "wb2": "wC"}[wbname]
            t = p.dma("sp", dst, src, deps=deps, group=grp, tag=f"{kind}{arg}.{part}")
            B(wbname).wrote(t)
            return t

        wrot = [0]
        _wbs = ((wb0, "wb0"), (wb1, "wb1"), (wb2, "wb2"))

        def next_w():
            i = wrot[0] % 3
            wrot[0] += 1
            return _wbs[i]

        def layer_norm(t0, ntok, gamma_fused, bcast_pe=False, sq_eng="act"):
            for n0 in range(t0, t0 + ntok, 512):
                ln_block(n0, gamma_fused, bcast_pe, sq_eng)

        def ln_block(n0, gamma_fused, bcast_pe=False, sq_eng="act"):
            if True:
                ta = tb = None
                for cc in range(CC):
                    t_sa = p.emit("pe", lambda e, cc=cc, n0=n0: e.matmul(
                        ps[6][0:1, :], onesf[:, :], rsl(cc, n0), start=(cc == 0), stop=(cc == CC - 1)),
                        deps=B("R").deps_read(rslot(cc, n0)) + B("onesf").deps_read() +
                             (B("ps6").deps_write() if cc == 0 else []),
                        tag="sumx")
                    B("R").read(t_sa, rslot(cc, n0))
                    B("onesf").read(t_sa)
                    ta = t_sa
                for cc in range(CC):
                    pr = cc % 2
                    if sq_eng == "act":
                        sqf = lambda e, cc=cc, n0=n0, pr=pr: e.activation(
                            sqt[:, pr * 512:(pr + 1) * 512], rsl(cc, n0), AF.Square)
                    else:
                        sqf = lambda e, cc=cc, n0=n0, pr=pr: e.tensor_mul(
                            sqt[:, pr * 512:(pr + 1) * 512], rsl(cc, n0), rsl(cc, n0))
                    t_sq = p.emit(sq_eng, sqf,
                        deps=B("R").deps_read(rslot(cc, n0)) + B("sqt").deps_write(pr), tag="sq")
                    B("sqt").wrote(t_sq, pr)
                    B("R").read(t_sq, rslot(cc, n0))
                    t_sb2 = p.emit("pe", lambda e, cc=cc, pr=pr: e.matmul(
                        ps[7][0:1, :], onesc[:, :], sqt[:, pr * 512:(pr + 1) * 512],
                        start=(cc == 0), stop=(cc == CC - 1)),
                        deps=[t_sq] + B("onesc").deps_read() + (B("ps7").deps_write() if cc == 0 else []),
                        tag="sumx2")
                    B("sqt").read(t_sb2, pr)
                    B("onesc").read(t_sb2)
                    tb = t_sb2
                B("ps6").wrote(ta)
                B("ps7").wrote(tb)
                t_ea = p.emit("act", lambda e: e.mul(srow_a[:, :], ps[6][0:1, :], 1.0 / DM),
                              deps=[ta] + B("srow_a").deps_write(), tag="mu")
                B("srow_a").wrote(t_ea)
                B("ps6").read(t_ea)
                t_eb = p.emit("act", lambda e: e.mul(srow_b[:, :], ps[7][0:1, :], 1.0 / DM),
                              deps=[tb] + B("srow_b").deps_write(), tag="ex2")
                B("srow_b").wrote(t_eb)
                B("ps7").read(t_eb)
                t_c = p.emit("dve", lambda e: e.tensor_mul(srow_c[:, :], srow_a[:, :], srow_a[:, :]),
                             deps=[t_ea] + B("srow_c").deps_write(), tag="mu2")
                B("srow_c").wrote(t_c)
                B("srow_a").read(t_c)
                t_v = p.emit("dve", lambda e: e.tensor_sub(srow_b[:, :], srow_b[:, :], srow_c[:, :]),
                             deps=[t_eb, t_c], tag="var")
                B("srow_b").wrote(t_v)
                B("srow_c").read(t_v)
                eps = 1e-30 if gamma_fused else LN_EPS
                t_ve = p.emit("dve", lambda e, eps=eps: e.tensor_scalar_add(
                    srow_b[:, :], srow_b[:, :], float(eps)), deps=[t_v], tag="veps")
                B("srow_b").wrote(t_ve)
                t_ln = p.emit("act", lambda e: e.activation(
                    srow_b[:, :], srow_b[:, :], AF.Ln), deps=[t_ve], tag="lnv")
                t_rsd = p.emit("act", lambda e: e.activation(
                    srow_b[:, :], srow_b[:, :], AF.Exp, scale=-0.5), deps=[t_ln], tag="rstd")
                B("srow_b").wrote(t_rsd)
                t_mr = p.emit("dve", lambda e: e.tensor_mul(srow_c[:, :], srow_a[:, :], srow_b[:, :]),
                              deps=[t_rsd, t_ea] + B("srow_c").deps_write(), tag="murstd")
                B("srow_c").wrote(t_mr)
                B("srow_a").read(t_mr)
                B("srow_b").read(t_mr)
                if bcast_pe:
                    t_b1m = p.emit("pe", lambda e: e.matmul(ps[6][:, :], onesr[:, :], srow_b[:, :],
                                                            start=True, stop=True),
                                   deps=[t_rsd] + B("ps6").deps_write() + B("onesr").deps_read(),
                                   tag="bcp1")
                    B("ps6").wrote(t_b1m)
                    B("srow_b").read(t_b1m)
                    B("onesr").read(t_b1m)
                    t_bc1 = p.emit("act", lambda e: e.copy(rep_a[:, :], ps[6][:, :]),
                                   deps=[t_b1m] + B("rep_a").deps_write(), tag="bcp1e")
                    B("ps6").read(t_bc1)
                    B("rep_a").wrote(t_bc1)
                    t_mr2 = B("srow_c").deps_read()
                    t_b2m = p.emit("pe", lambda e: e.matmul(ps[7][:, :], negr[:, :], srow_c[:, :],
                                                            start=True, stop=True),
                                   deps=[t_mr] + B("ps7").deps_write() + B("negr").deps_read(),
                                   tag="bcp2")
                    B("ps7").wrote(t_b2m)
                    B("srow_c").read(t_b2m)
                    B("negr").read(t_b2m)
                    t_bc2 = p.emit("act", lambda e: e.copy(rep_b[:, :], ps[7][:, :]),
                                   deps=[t_b2m] + B("rep_b").deps_write(), tag="bcp2e")
                    B("ps7").read(t_bc2)
                    B("rep_b").wrote(t_bc2)
                else:
                    t_neg = p.emit("dve", lambda e: e.tensor_scalar_mul(srow_c[:, :], srow_c[:, :], -1.0),
                                   deps=[t_mr], tag="negmur")
                    B("srow_c").wrote(t_neg)
                    t_o1 = p.dma("act", browL[0:1, 0:512], srow_b[:, :], deps=[t_rsd],
                                 group="browL", tag="brL1o")
                    B("srow_b").read(t_o1)
                    t_o2 = p.dma("act", browL[0:1, 512:1024], srow_c[:, :], deps=[t_neg],
                                 group="browL", tag="brL2o")
                    B("srow_c").read(t_o2)
                    t_bc1 = p.dma("act", rep_a[:, :], browL[0:1, 0:512].broadcast_to([128, 512]),
                                  deps=[t_o1] + B("rep_a").deps_write(), group="browL", tag="brL1i")
                    B("rep_a").wrote(t_bc1)
                    t_bc2 = p.dma("act", rep_b[:, :], browL[0:1, 512:1024].broadcast_to([128, 512]),
                                  deps=[t_o2] + B("rep_b").deps_write(), group="browL", tag="brL2i")
                    B("rep_b").wrote(t_bc2)
                ap_eng = "pool" if (sq_eng == "pool" and not gamma_fused) else "dve"
                for cc in range(CC):
                    xslot = rslot(cc, n0)
                    if not gamma_fused:
                        t_p1 = p.emit(ap_eng, lambda e, cc=cc, n0=n0: e.tensor_mul(
                            ln_t[:, :], rsl(cc, n0), rep_a[:, :]),
                            deps=[t_bc1] + B("f32t").deps_write() + B("R").deps_read(rslot(cc, n0)),
                            tag="lnap1")
                        t_p2f = lambda e, cc=cc, n0=n0: e.tensor_add(
                            xsl(cc, n0), rep_b[:, :], ln_t[:, :])
                    else:
                        t_p1 = p.emit(ap_eng, lambda e, cc=cc, n0=n0: e.scalar_tensor_tensor(
                            ln_t[:, :], rsl(cc, n0), gamc[:, cc:cc + 1], rep_a[:, :],
                            ALU.mult, ALU.mult),
                            deps=[t_bc1] + B("f32t").deps_write() + B("R").deps_read(rslot(cc, n0)) +
                                 B("gamc").deps_read(), tag="lnap1g")
                        t_p2f = lambda e, cc=cc, n0=n0: e.scalar_tensor_tensor(
                            xsl(cc, n0), rep_b[:, :], gamc[:, cc:cc + 1], ln_t[:, :],
                            ALU.mult, ALU.add)
                    B("rep_a").read(t_p1)
                    B("R").read(t_p1, rslot(cc, n0))
                    B("f32t").wrote(t_p1)
                    t_p2 = p.emit(ap_eng, t_p2f,
                                  deps=[t_bc2, t_p1] + B("xn").deps_write(xslot), tag="lnap2")
                    B("rep_b").read(t_p2)
                    B("f32t").read(t_p2)
                    B("xn").wrote(t_p2, xslot)

        def lora_prep(part=0):
            xdeps = B("xn").deps_read([rslot(cc, 0) for cc in range(CC)])
            p6b = ps[6][:, :].bitcast(BF16)
            lt_toks = []
            for cc in (() if part == 2 else range(CC)):
                t_tr = p.emit("pe", lambda e, cc=cc: e.transpose(
                    p6b[0:16, 0:128], xn[:, cc * S + 1: cc * S + 17], ident[:, :]),
                    deps=xdeps + B("ps6").deps_write() + B("ident").deps_read(), tag="lt_tr")
                B("ps6").wrote(t_tr)
                B("xn").read(t_tr, rslot(cc, 0))
                B("ident").read(t_tr)
                t_ev = p.emit("act", lambda e, cc=cc: e.copy(
                    lt[:, cc * 128:(cc + 1) * 128], p6b[0:16, 0:128]),
                    deps=[t_tr] + (B("lt").deps_write() if cc == 0 else []), tag="lt_ev")
                B("ps6").read(t_ev)
                if cc == 0:
                    B("lt").wrote(t_ev)
                else:
                    B("lt").accum(t_ev)
                lt_toks.append(t_ev)
            if part == 2:
                lt_toks = list(B("lt").deps_read())
            sel_list = ((o1sc, "o1sc", o1c, "o1c"), (o2sc, "o2sc", o2c, "o2c"))
            if part == 1:
                sel_list = sel_list[:1]
            elif part == 2:
                sel_list = sel_list[1:]
            for sel_t, sel_n, dst_t, dst_n in sel_list:
                mms = []
                for cc in range(CC):
                    t_mm = p.emit("pe", lambda e, cc=cc, sel_t=sel_t: e.matmul(
                        ps[7][0:128, cc * 4:(cc + 1) * 4], lt[:, cc * 128:(cc + 1) * 128],
                        sel_t[:, :], start=(cc == 0), stop=(cc == CC - 1)),
                        deps=lt_toks + B(sel_n).deps_read() +
                             (B("ps7").deps_write() if cc == 0 else []), tag=dst_n + "mm")
                    B("lt").read(t_mm)
                    B(sel_n).read(t_mm)
                    mms.append(t_mm)
                B("ps7").wrote(mms[-1])
                t_ev = p.emit("act", lambda e, dst_t=dst_t: e.copy(
                    dst_t[:, :], ps[7][0:128, 0:32]),
                    deps=[mms[-1]] + B(dst_n).deps_write(), tag=dst_n + "ev")
                B("ps7").read(t_ev)
                B(dst_n).wrote(t_ev)
            for half in (() if part == 2 else range(2)):
                t_mm = p.emit("pe", lambda e, half=half: e.matmul(
                    ps[7][0:4, :], e1s[:, :], lt[:, half * 512:(half + 1) * 512],
                    start=True, stop=True),
                    deps=lt_toks + B("ps7").deps_write() + B("e1s").deps_read(), tag="e1mm")
                B("ps7").wrote(t_mm)
                B("lt").read(t_mm)
                B("e1s").read(t_mm)
                t_ev = p.emit("act", lambda e: e.copy(t1[0:4, 0:512], ps[7][0:4, 0:512]),
                              deps=[t_mm] + B("t1").deps_write(), tag="e1ev")
                B("ps7").read(t_ev)
                B("t1").wrote(t_ev)
                for i in range(4):
                    t_cp = p.dma("act", e1z[i:i + 1, i * DM + half * 512: i * DM + (half + 1) * 512],
                                 t1[i:i + 1, 0:512], deps=[t_ev] + B("e1z").deps_write(),
                                 group="e1z", tag="e1cp")
                    B("t1").read(t_cp)
                    B("e1z").accum(t_cp)
            if part == 1:
                return
            e24_toks = []
            for half in range(2):
                t_mm = p.emit("pe", lambda e, half=half: e.matmul(
                    ps[7][0:4, :], e2s[:, :], lt[:, half * 512:(half + 1) * 512],
                    start=True, stop=True),
                    deps=lt_toks + B("ps7").deps_write() + B("e2s").deps_read(), tag="e2mm")
                B("ps7").wrote(t_mm)
                B("lt").read(t_mm)
                B("e2s").read(t_mm)
                t_ev = p.emit("act", lambda e, half=half: e.copy(
                    e24[:, half * 512:(half + 1) * 512], ps[7][0:4, 0:512]),
                    deps=[t_mm] + (B("u4").deps_write() if half == 0 else []), tag="e2ev")
                B("ps7").read(t_ev)
                if half == 0:
                    B("u4").wrote(t_ev)
                else:
                    B("u4").accum(t_ev)
                e24_toks.append(t_ev)
            t_out = p.dma("act", e2scr, e24[:, :], deps=e24_toks, group="e2b", tag="e2out")
            B("u4").read(t_out)
            ins = {}
            for i in range(4):
                for bb in range(2):
                    oc = 2 * i + bb
                    t_in = p.dma("act", e2st[:, oc * 4:(oc + 1) * 4],
                                 e2scr[i:i + 1, bb * 512:(bb + 1) * 512].rearrange(
                                     "o (p j) -> o p j", p=128)[0],
                                 deps=[t_out] + B("e2st").deps_write(), group=f"e2b{oc % 2}",
                                 tag="e2in")
                    B("e2st").wrote(t_in)
                    ins[oc] = t_in
            for g in range(2):
                trs = []
                for ocl in range(4):
                    oc = g * 4 + ocl
                    t_tr = p.emit("pe", lambda e, oc=oc, ocl=ocl: e.transpose(
                        p6b[0:4, ocl * 128:(ocl + 1) * 128], e2st[:, oc * 4:(oc + 1) * 4],
                        ident[:, :]),
                        deps=[ins[oc]] + B("ident").deps_read() +
                             (B("ps6").deps_write() if ocl == 0 else []), tag="e2tr")
                    B("e2st").read(t_tr)
                    trs.append(t_tr)
                B("ps6").wrote(trs[-1])
                t_ev = p.emit("act", lambda e, g=g: e.copy(
                    e2d[:, g * 512:(g + 1) * 512], p6b[0:4, 0:512]),
                    deps=trs + B("e2d").deps_write(), tag="e2dev")
                B("ps6").read(t_ev)
                if g == 0:
                    B("e2d").wrote(t_ev)
                else:
                    B("e2d").accum(t_ev)

        def t1_compute(t0):
            for nh in range(2):
                n0 = t0 + nh * 512
                last = None
                for cc in range(CC):
                    t_mm = p.emit("pe", lambda e, cc=cc, n0=n0: e.matmul(
                        ps[7][0:4, :], o1c[:, cc * 4:(cc + 1) * 4], xsl(cc, n0),
                        start=(cc == 0), stop=(cc == CC - 1)),
                        deps=B("xn").deps_read(rslot(cc, n0)) + B("o1c").deps_read() +
                             (B("ps7").deps_write() if cc == 0 else []), tag="t1mm")
                    B("xn").read(t_mm, rslot(cc, n0))
                    B("o1c").read(t_mm)
                    last = t_mm
                B("ps7").wrote(last)
                t_ev = p.emit("act", lambda e, nh=nh: e.copy(
                    t1[:, nh * 512:(nh + 1) * 512], ps[7][0:4, 0:512]),
                    deps=[last] + (B("t1").deps_write() if nh == 0 else []), tag="t1ev")
                B("ps7").read(t_ev)
                if nh == 0:
                    B("t1").wrote(t_ev)
                else:
                    B("t1").accum(t_ev)

        def ffn(store_seq=None, next_seq=None, hook=None, ln_pre=(), lora_pre=False, t1_pre=(), lora2_first_tile=False, prologue_pe_bcast=False):
            for ti in range(2):
                t0 = ti * NT
                if ti not in ln_pre:
                    layer_norm(t0, NT, False, bcast_pe=prologue_pe_bcast)
                if ti == 0 and prologue_pe_bcast:
                    layer_norm(NT, NT, False, bcast_pe=True)
                if ti == 0 and not lora_pre:
                    lora_prep(part=1 if lora2_first_tile else 0)
                if ti not in t1_pre:
                    t1_compute(t0)
                for g in range(32):
                    wt, wbn = next_w()
                    t_w = wslab("w1", g, wt, wbn)
                    for hcl in range(1):
                        hc = g
                        for nh in range(2):
                            n0 = t0 + nh * 512
                            bi4 = (hc % 2) * 2 + nh
                            bank, bn = ps[bi4], f"ps{bi4}"
                            for cc in range(CC):
                                t_mm = p.emit("pe", lambda e, cc=cc, n0=n0, wt=wt, bank=bank: e.matmul(
                                    bank[:, :], wt[:, cc * 128:(cc + 1) * 128],
                                    xsl(cc, n0), start=(cc == 0), stop=False),
                                    deps=[t_w] + B("xn").deps_read(rslot(cc, n0)) +
                                         (B(bn).deps_write() if cc == 0 else []), tag=f"h{hc}")
                                B(wbn).read(t_mm)
                                B("xn").read(t_mm, rslot(cc, n0))
                                if cc == 0:
                                    B(bn).wrote(t_mm)
                                else:
                                    B(bn).accum(t_mm)
                            t_c1 = p.emit("pe", lambda e, hc=hc, nh=nh, bank=bank: e.matmul(
                                bank[:, :], e1z[:, hc * 128:(hc + 1) * 128],
                                t1[:, nh * 512:(nh + 1) * 512], start=False, stop=True),
                                deps=B("e1z").deps_read() + B("t1").deps_read(), tag=f"c1_{hc}")
                            B("e1z").read(t_c1)
                            B("t1").read(t_c1)
                            B(bn).accum(t_c1)
                            t_g = p.emit("act", lambda e, hc=hc, nh=nh, bank=bank: e.activation(
                                h_act(hc, nh), bank[:, :], AF.Gelu, bias=b1c[:, hc:hc + 1]),
                                deps=[t_c1] + B("arena").deps_write(h_slot(hc)) + B("b1c").deps_read(),
                                tag=f"gelu{hc}")
                            B(bn).read(t_g)
                            B("b1c").read(t_g)
                            if nh == 0:
                                B("arena").wrote(t_g, h_slot(hc))
                            else:
                                B("arena").accum(t_g, h_slot(hc))
                if ti == 0 and lora2_first_tile:
                    lora_prep(part=2)
                for nh in range(2):
                    uev = []
                    for j in range(4):
                        bank, bn = ps[2 + j], f"ps{2 + j}"
                        last = None
                        for qc in range(CC):
                            hc = 8 * j + qc
                            t_mm = p.emit("pe", lambda e, hc=hc, qc=qc, nh=nh, bank=bank: e.matmul(
                                bank[0:4, :], o2c[:, qc * 4:(qc + 1) * 4], h_act(hc, nh),
                                start=(qc == 0), stop=(qc == CC - 1)),
                                deps=B("arena").deps_read(h_slot(hc)) + B("o2c").deps_read() +
                                     (B(bn).deps_write() if qc == 0 else []), tag=f"u{j}")
                            B("o2c").read(t_mm)
                            B("arena").read(t_mm, h_slot(hc))
                            last = t_mm
                        B(bn).wrote(last)
                        t_ev = p.emit("act", lambda e, j=j, bank=bank: e.copy(
                            u4[:, j * 512:(j + 1) * 512], bank[0:4, 0:512]),
                            deps=[last] + (B("u4").deps_write() if j == 0 else []), tag="uev")
                        B(bn).read(t_ev)
                        if j == 0:
                            B("u4").wrote(t_ev)
                        else:
                            B("u4").accum(t_ev)
                        uev.append(t_ev)
                    t_uo = p.dma("act", uscr, u4[:, :], deps=uev, group="ub", tag="u_out")
                    B("u4").read(t_uo)
                    t_uin = p.dma("act", u4[:, :].rearrange("j (i s) -> j i s", i=4),
                                  uscr.rearrange("i (j s) -> j i s", j=4),
                                  deps=[t_uo], group="ub", tag="u_in")
                    B("u4").wrote(t_uin)
                    for g in range(32):
                        wt, wbn = next_w()
                        t_w2l = wslab("w2", g, wt, wbn)
                        for oc in range(CC):
                            bank, bn = ps[oc], f"ps{oc}"
                            for hcl in range(1):
                                hc = g
                                first = (hc == 0)
                                t_mm = p.emit("pe", lambda e, oc=oc, hc=hc, nh=nh, wt=wt, bank=bank: e.matmul(
                                    bank[:, :], wt[:, oc * 128:(oc + 1) * 128],
                                    h_act(hc, nh), start=(hc == 0), stop=False),
                                    deps=[t_w2l] + B("arena").deps_read(h_slot(hc)) +
                                         (B(bn).deps_write() if first else []), tag=f"o{oc}")
                                B(wbn).read(t_mm)
                                B("arena").read(t_mm, h_slot(hc))
                                if first:
                                    B(bn).wrote(t_mm)
                                else:
                                    B(bn).accum(t_mm)
                    for oc in (6, 7, 0, 1, 2, 3, 4, 5):
                        bank, bn = ps[oc], f"ps{oc}"
                        i = oc // 2
                        t_c2 = p.emit("pe", lambda e, oc=oc, i=i, bank=bank: e.matmul(
                            bank[:, :], e2d[:, oc * 128:(oc + 1) * 128], u4[:, i * 512:(i + 1) * 512],
                            start=False, stop=True),
                            deps=[t_uin] + B("e2d").deps_read(), tag=f"c2_{oc}")
                        B("e2d").read(t_c2)
                        B("u4").read(t_c2)
                        B(bn).accum(t_c2)
                        t_t = p.emit("act", lambda e, oc=oc, bank=bank: e.add(
                            tmp_o[:, :], bank[:, :], b2c[:, oc:oc + 1]),
                            deps=[t_c2] + B("f32t").deps_write() + B("b2c").deps_read(),
                            tag=f"badd{oc}")
                        B(bn).read(t_t)
                        B("b2c").read(t_t)
                        B("f32t").wrote(t_t)
                        n0 = t0 + nh * 512
                        t_r = p.emit("dve", lambda e, oc=oc, n0=n0: e.scalar_tensor_tensor(
                            rsl(oc, n0), tmp_o[:, :], 0.5, rsl(oc, n0), ALU.mult, ALU.add),
                            deps=[t_t] + B("R").deps_write(rslot(oc, n0)), tag=f"radd{oc}")
                        B("f32t").read(t_r)
                        B("R").wrote(t_r, rslot(oc, n0))
                        if store_seq is not None:
                            t_st = p.dma("pool", outT[store_seq][oc][:, n0:n0 + 512],
                                         rsl(oc, n0), deps=[t_r], group=f"io{oc % 4}",
                                         tag=f"stR{store_seq}_{oc}_{n0}")
                            B("R").read(t_st, rslot(oc, n0))
                            st_toks.append(t_st)
                            if next_seq is not None:
                                t_ld = p.dma("pool", rsl(oc, n0), xT[next_seq][oc][:, n0:n0 + 512],
                                             deps=[t_st] + B("R").deps_write(rslot(oc, n0)),
                                             group=f"io{oc % 4}", tag=f"ldR{next_seq}_{oc}_{n0}")
                                B("R").wrote(t_ld, rslot(oc, n0))
                    if hook is not None:
                        hook(ti, nh)

        def attn(ln_pre_blocks=0, hook=None):
            for n0 in range(ln_pre_blocks * 512, S, 512):
                ln_block(n0, True)
            for hh in range(NHALF):
                h0 = hh * HPH
                def mhat_head(h):
                    for which, reg0, coloff in (("q", QT0, 0), ("k", KT0, 16)):
                        for nb in range(4):
                            col0 = reg0 + h * S + nb * 512
                            pr = nb % 2
                            t_sq = p.emit("pool", lambda e, col0=col0, pr=pr: e.tensor_mul(
                                sqt[:, pr * 512:(pr + 1) * 512], arena[:, col0:col0 + 512],
                                arena[:, col0:col0 + 512]),
                                deps=B("arena").deps_read(aslot(col0, 512)) + B("sqt").deps_write(pr),
                                tag="nsq")
                            B("sqt").wrote(t_sq, pr)
                            B("arena").read(t_sq, aslot(col0, 512))
                            t_mm = p.emit("pe", lambda e, pr=pr: e.matmul(
                                ps[6][0:1, :], onesc[:, :], sqt[:, pr * 512:(pr + 1) * 512],
                                start=True, stop=True),
                                deps=[t_sq] + B("ps6").deps_write(), tag="nmm")
                            B("ps6").wrote(t_mm)
                            B("sqt").read(t_mm, pr)
                            ncol = coloff + h * 4 + nb
                            t_rm = p.emit("dve", lambda e, ncol=ncol: e.reduce_max(
                                nrm[0:1, ncol:ncol + 1], ps[6][0:1, :], AX.X),
                                deps=[t_mm] + B("nrm").deps_write(), tag="nmax")
                            B("ps6").read(t_rm)
                            B("nrm").accum(t_rm)
                    t_q = p.emit("dve", lambda e, h=h: e.reduce_max(
                        nrm[0:1, 32 + h:33 + h], nrm[0:1, h * 4:(h + 1) * 4], AX.X),
                        deps=B("nrm").deps_read(), tag="nq")
                    B("nrm").accum(t_q)
                    t_k = p.emit("dve", lambda e, h=h: e.reduce_max(
                        nrm[0:1, 40 + h:41 + h], nrm[0:1, 16 + h * 4: 16 + (h + 1) * 4], AX.X),
                        deps=B("nrm").deps_read(), tag="nk")
                    B("nrm").accum(t_k)
                    t_pr = p.emit("dve", lambda e, h=h: e.tensor_mul(
                        nrm[0:1, 48 + h:49 + h], nrm[0:1, 32 + h:33 + h], nrm[0:1, 40 + h:41 + h]),
                        deps=[t_q, t_k], tag="nprod")
                    B("nrm").accum(t_pr)
                    t_sr = p.emit("act", lambda e, h=h: e.activation(
                        nrm[0:1, 48 + h:49 + h], nrm[0:1, 48 + h:49 + h], AF.Sqrt),
                        deps=[t_pr], tag="nsqrt")
                    B("nrm").accum(t_sr)
                    t_nm = p.emit("dve", lambda e, h=h: e.tensor_scalar(
                        nrm[0:1, 56 + h:57 + h], nrm[0:1, 48 + h:49 + h], -1.01, float(CUSH),
                        ALU.mult, ALU.add),
                        deps=[t_sr], tag="negm")
                    B("nrm").accum(t_nm)
                    t_no = p.dma("act", browN[0:1, h:h + 1], nrm[0:1, 56 + h:57 + h],
                                 deps=[t_nm], group="browN", tag="brNo")
                    B("nrm").read(t_no)
                    t_ev = p.dma("act", negm[:, h:h + 1],
                                 browN[0:1, h:h + 1].broadcast_to([128, 1]),
                                 deps=[t_no] + B("negm").deps_write(h), group="browN", tag="brNi")
                    B("negm").wrote(t_ev, h)
                def qkv_slab(kind, hh, col0):
                    gate = prep_done["wq"][1]
                    f0 = {"wq": 0, "wk": 1, "wv": 2}[kind] * DM + hh * 512
                    src = wqs[:, f0:f0 + 512].rearrange("(cc p) f -> p cc f", p=128)
                    dst = arena[:, col0:col0 + 4096].rearrange("p (cc f) -> p cc f", cc=CC)
                    sl = aslot(col0, 4096)
                    t = p.dma("sp", dst, src, deps=B("arena").deps_write(sl) + [gate],
                              group="wA", tag=f"{kind}{hh}")
                    B("arena").wrote(t, sl)
                    return t, sl

                t_wv, wv_sl = qkv_slab("wv", hh, QT0)
                for tt in range(16):
                    bank, bn = ps[tt % 2], f"ps{tt % 2}"
                    last = None
                    for cc in range(CC):
                        t_mm = p.emit("pe", lambda e, cc=cc, tt=tt, bank=bank: e.matmul(
                            bank[:, :], xn[:, cc * S + tt * 128: cc * S + (tt + 1) * 128],
                            arena[:, QT0 + cc * 512: QT0 + (cc + 1) * 512],
                            start=(cc == 0), stop=(cc == CC - 1)),
                            deps=[t_wv] + B("xn").deps_read(rslot(cc, tt * 128)) +
                                 (B(bn).deps_write() if cc == 0 else []), tag="wvmm")
                        B("arena").read(t_mm, wv_sl)
                        B("xn").read(t_mm, rslot(cc, tt * 128))
                        if cc == 0:
                            B(bn).wrote(t_mm)
                        else:
                            B(bn).accum(t_mm)
                        last = t_mm
                    col0 = V0 + tt * 512
                    slots = aslot(col0, 512)
                    t_ev = p.emit("dve", lambda e, col0=col0, bank=bank: e.tensor_copy(
                        arena[:, col0:col0 + 512], bank[:, :]),
                        deps=[last] + B("arena").deps_write(slots), tag="wvev")
                    B(bn).read(t_ev)
                    B("arena").wrote(t_ev, slots)
                def qk_head(h):
                    for kind, reg0, scale_ev in (("wq", QT0, SCALE_ATTN),
                                                 ("wk", KT0, None)):
                        wt, wbn = next_w()
                        wtok = wslab(kind, hh, wt, wbn, part=h)
                        for nb in range(4):
                            bank, bn = ps[nb % 2], f"ps{nb % 2}"
                            last = None
                            for cc in range(CC):
                                t_mm = p.emit("pe", lambda e, cc=cc, nb=nb, wt=wt, bank=bank: e.matmul(
                                    bank[:, :], wt[:, cc * 128:(cc + 1) * 128],
                                    xsl(cc, nb * 512), start=(cc == 0), stop=(cc == CC - 1)),
                                    deps=[wtok] + B("xn").deps_read(rslot(cc, nb * 512)) +
                                         (B(bn).deps_write() if cc == 0 else []), tag=kind + "mm")
                                B(wbn).read(t_mm)
                                B("xn").read(t_mm, rslot(cc, nb * 512))
                                if cc == 0:
                                    B(bn).wrote(t_mm)
                                else:
                                    B(bn).accum(t_mm)
                                last = t_mm
                            col0 = reg0 + h * S + nb * 512
                            slots = aslot(col0, 512)
                            if scale_ev is None:
                                t_ev = p.emit("act", lambda e, col0=col0, bank=bank: e.copy(
                                    arena[:, col0:col0 + 512], bank[:, :]),
                                    deps=[last] + B("arena").deps_write(slots), tag=kind + "ev")
                            else:
                                t_ev = p.emit("act", lambda e, col0=col0, bank=bank, scale_ev=scale_ev: e.mul(
                                    arena[:, col0:col0 + 512], bank[:, :], float(scale_ev)),
                                    deps=[last] + B("arena").deps_write(slots), tag=kind + "ev")
                            B(bn).read(t_ev)
                            B("arena").wrote(t_ev, slots)
                units = [(h, qb) for h in range(HPH) for qb in range(4)]
                gkt = [0]

                def emit_S_exp(h, qb, kt):
                    sb_i = (0, 1, 7)[gkt[0] % 3]
                    bank, bn = ps[sb_i], f"ps{sb_i}"
                    pslot = gkt[0] % 4
                    gkt[0] += 1
                    diag = (kt // 4 == qb)
                    dep0 = B(bn).deps_write()
                    if diag:
                        m = kt - 4 * qb
                        t_pl = p.emit("pe", lambda e, m=m, bank=bank: e.matmul(
                            bank[:, :], ident[:, :], maskm[:, 384 - 128 * m: 896 - 128 * m],
                            start=True, stop=False),
                            deps=dep0 + B("maskm").deps_read() + B("ident").deps_read(),
                            tag="spl")
                        B(bn).wrote(t_pl)
                        B("maskm").read(t_pl)
                        B("ident").read(t_pl)
                        dep0 = []
                    kcol = KT0 + h * S + kt * 128
                    qcol = QT0 + h * S + qb * 512
                    t_s = p.emit("pe", lambda e, kcol=kcol, qcol=qcol, bank=bank, diag=diag: e.matmul(
                        bank[:, :], arena[:, kcol:kcol + 128], arena[:, qcol:qcol + 512],
                        start=not diag, stop=True),
                        deps=dep0 + B("arena").deps_read(aslot(kcol, 128) + aslot(qcol, 512)),
                        tag="smm")
                    if diag:
                        B(bn).accum(t_s)
                    else:
                        B(bn).wrote(t_s)
                    B("arena").read(t_s, aslot(kcol, 128) + aslot(qcol, 512))
                    t_e = p.emit("act", lambda e, pslot=pslot, h=h, bank=bank: e.activation(
                        pb[:, pslot * 512:(pslot + 1) * 512], bank[:, :], AF.Exp,
                        bias=negm[:, h:h + 1]),
                        deps=[t_s] + B("pb").deps_write(pslot) + B("negm").deps_read(h),
                        tag="exp")
                    B(bn).read(t_e)
                    B("negm").read(t_e, h)
                    B("pb").wrote(t_e, pslot)
                    return t_e, pslot

                def emit_rspv(h, qb, kt, t_e, pslot, first, last, ubank):
                    rsb, rsn = (ps[3], "ps3") if ubank == 0 else (ps[6], "ps6")
                    ob, obn = (ps[2], "ps2") if ubank == 0 else (ps[5], "ps5")
                    t_rs = p.emit("pe", lambda e, pslot=pslot, first=first, last=last, rsb=rsb: e.matmul(
                        rsb[0:1, :], onesc[:, :], pb[:, pslot * 512:(pslot + 1) * 512],
                        start=first, stop=last),
                        deps=[t_e] + (B(rsn).deps_write() if first else []), tag="rsmm")
                    B("pb").read(t_rs, pslot)
                    if first:
                        B(rsn).wrote(t_rs)
                    else:
                        B(rsn).accum(t_rs)
                    vcol = V0 + kt * 512 + h * 128
                    t_pv = p.emit("pe", lambda e, vcol=vcol, pslot=pslot, first=first, last=last, ob=ob: e.matmul(
                        ob[:, :], arena[:, vcol:vcol + 128], pb[:, pslot * 512:(pslot + 1) * 512],
                        start=first, stop=last),
                        deps=[t_e] + B("arena").deps_read(aslot(vcol, 128)) +
                             (B(obn).deps_write() if first else []), tag="pvmm")
                    B("pb").read(t_pv, pslot)
                    B("arena").read(t_pv, aslot(vcol, 128))
                    if first:
                        B(obn).wrote(t_pv)
                    else:
                        B(obn).accum(t_pv)

                def emit_tail(h, qb, ubank):
                    rsb, rsn = (ps[3], "ps3") if ubank == 0 else (ps[6], "ps6")
                    ob, obn = (ps[2], "ps2") if ubank == 0 else (ps[5], "ps5")
                    t_l = p.emit("act", lambda e, rsb=rsb: e.activation(recr[:, :], rsb[0:1, :], AF.Ln),
                                 deps=B(rsn).deps_read() + B("srow_a").deps_write(), tag="rln")
                    B(rsn).read(t_l)
                    B("srow_a").wrote(t_l)
                    t_x = p.emit("act", lambda e: e.activation(recr[:, :], recr[:, :], AF.Exp, scale=-1.0),
                                 deps=[t_l], tag="rexp")
                    B("srow_a").wrote(t_x)
                    t_ro = p.dma("act", browF, recr[:, :], deps=[t_x], group="browF", tag="brFo")
                    B("srow_a").read(t_ro)
                    t_rb2 = p.dma("act", f32t[:, :], browF[0:1, :].broadcast_to([128, 512]),
                                  deps=[t_ro] + B("f32t").deps_write(), group="browF", tag="brFi")
                    B("f32t").wrote(t_rb2)
                    ocol = OA0 + h * S + qb * 512
                    t_oa = p.emit("dve", lambda e, ocol=ocol, ob=ob: e.tensor_mul(
                        arena[:, ocol:ocol + 512], ob[:, :], f32t[:, :]),
                        deps=[t_rb2] + B(obn).deps_read() + B("arena").deps_write(aslot(ocol, 512)),
                        tag="oaev")
                    B(obn).read(t_oa)
                    B("f32t").read(t_oa)
                    B("arena").wrote(t_oa, aslot(ocol, 512))

                pending = []
                pre = {}
                DEPTH = 2
                fstate = {"i": 0}
                qk_done = set()

                def emit_flash_unit():
                    ui = fstate["i"]
                    fstate["i"] += 1
                    h, qb = units[ui]
                    nkt = qb * 4 + 4
                    ubank = ui % 2

                    def push(item):
                        pending.append(item)
                        if len(pending) > DEPTH:
                            emit_rspv(*pending.pop(0))

                    done = pre.pop((h, qb), [])
                    for kt, t_e, pslot in done:
                        push((h, qb, kt, t_e, pslot, kt == 0, kt == nkt - 1, ubank))
                    for kt in range(len(done), nkt):
                        t_e, pslot = emit_S_exp(h, qb, kt)
                        push((h, qb, kt, t_e, pslot, kt == 0, kt == nkt - 1, ubank))
                    if ui + 1 < len(units) and units[ui + 1][0] in qk_done:
                        nh_, nqb_ = units[ui + 1]
                        lst = []
                        for kt2 in range(min(2, nqb_ * 4 + 4)):
                            te2, psl2 = emit_S_exp(nh_, nqb_, kt2)
                            lst.append((kt2, te2, psl2))
                        pre[(nh_, nqb_)] = lst
                    while pending:
                        emit_rspv(*pending.pop(0))
                    emit_tail(h, qb, ubank)

                for h in range(HPH):
                    qk_head(h)
                    qk_done.add(h)
                    mhat_head(h)
                    if h >= 1:
                        for _ in range(4):
                            emit_flash_unit()
                for _ in range(4):
                    emit_flash_unit()
                gate = prep_done["wo"][hh]
                src = wos[hh * 512:(hh + 1) * 512, :].rearrange("(fc p) c -> p fc c", p=128)
                wo_sl = aslot(KT0, 4096)
                dst = arena[:, KT0:KT0 + 4096].rearrange("p (fc c) -> p fc c", fc=4)
                t_wo = p.dma("sp", dst, src, deps=B("arena").deps_write(wo_sl) + [gate],
                             group="wA", tag=f"wo{hh}")
                B("arena").wrote(t_wo, wo_sl)
                for qb in range(4):
                    for oc in range(CC):
                        bank, bn = ps[5 + (oc % 3)], f"ps{5 + (oc % 3)}"
                        last = None
                        for fc in range(HPH):
                            ocol = OA0 + fc * S + qb * 512
                            t_mm = p.emit("pe", lambda e, fc=fc, oc=oc, ocol=ocol, bank=bank: e.matmul(
                                bank[:, :], arena[:, KT0 + fc * 1024 + oc * 128: KT0 + fc * 1024 + (oc + 1) * 128],
                                arena[:, ocol:ocol + 512], start=(fc == 0), stop=(fc == HPH - 1)),
                                deps=[t_wo] + B("arena").deps_read(aslot(ocol, 512)) +
                                     (B(bn).deps_write() if fc == 0 else []), tag="opmm")
                            B("arena").read(t_mm, wo_sl)
                            B("arena").read(t_mm, aslot(ocol, 512))
                            if fc == 0:
                                B(bn).wrote(t_mm)
                            else:
                                B(bn).accum(t_mm)
                            last = t_mm
                        n0 = qb * 512
                        t_r = p.emit("dve", lambda e, oc=oc, n0=n0, bank=bank: e.tensor_add(
                            rsl(oc, n0), rsl(oc, n0), bank[:, :]),
                            deps=[last] + B("R").deps_write(rslot(oc, n0)), tag="raddA")
                        B(bn).read(t_r)
                        B("R").wrote(t_r, rslot(oc, n0))
                    if hook is not None:
                        hook(hh, qb)

        def load_R_blocks(seq, b0, b1):
            for cc in range(CC):
                sl = [cc * 4 + b for b in range(b0, b1)]
                t = p.dma("sp", R[:, cc * S + b0 * 512: cc * S + b1 * 512],
                          xT[seq][cc][:, b0 * 512: b1 * 512],
                          deps=B("R").deps_write(sl), group=f"ioA{cc % 4}",
                          tag=f"ldR{seq}_{cc}_{b0}")
                B("R").wrote(t, sl)

        def store_R(seq):
            dst = outT[seq].rearrange("cc p s -> p cc s")
            src = R[:, :].rearrange("p (cc s) -> p cc s", cc=CC)
            t = p.dma("pool", dst, src, deps=B("R").deps_read() + B("R").deps_write(),
                      group="io0", tag=f"stR{seq}")
            B("R").read(t)
            return t

        st_toks = []
        _blocks = do_prep()
        load_R_blocks(0, 0, 1)
        load_R_blocks(0, 1, 2)
        load_R_blocks(0, 2, 4)
        emit_prep([b for b in _blocks if b[0] == "w1"])
        emit_prep([b for b in _blocks if b[0] != "w1"])
        for seq in range(BPC):
            if n_sub >= 3:
                first = (seq == 0)

                def ffn1_hook(ti, nh):
                    if ti == 0 and nh == 0:
                        if not first:
                            ln_block(NT, False, sq_eng='pool')
                            ln_block(NT + 512, False, sq_eng='pool')
                        t1_compute(NT)
                    elif ti == 1 and nh == 0:
                        ln_block(0, True, sq_eng='pool')
                        ln_block(512, True, sq_eng='pool')
                        ln_block(NT, True, sq_eng='pool')
                    elif ti == 1 and nh == 1:
                        ln_block(NT + 512, True, sq_eng='pool')

                ffn(hook=ffn1_hook, ln_pre=(1,) if first else (0, 1),
                    lora_pre=not first, t1_pre=(1,) if first else (0, 1),
                    lora2_first_tile=True, prologue_pe_bcast=first)

                def attn_hook(hh, qb):
                    if hh != NHALF - 1:
                        return
                    if qb == 0:
                        ln_block(0, False, sq_eng='pool')
                    elif qb == 1:
                        ln_block(512, False, sq_eng='pool')
                    elif qb == 2:
                        lora_prep(part=1)
                        ln_block(NT, False, sq_eng='pool')
                    elif qb == 3:
                        t1_compute(0)
                        ln_block(NT + 512, False, sq_eng='pool')

                nxt = seq + 1 if seq + 1 < BPC else None

                def ffn2_hook(ti, nh):
                    if ti == 0 and nh == 0:
                        t1_compute(NT)
                    if nxt is not None:
                        if ti == 0 and nh == 1:
                            ln_block(0, False, sq_eng='pool')
                        elif ti == 1 and nh == 0:
                            ln_block(512, False, sq_eng='pool')
                            lora_prep(part=1)
                            t1_compute(0)

                attn(ln_pre_blocks=4, hook=attn_hook)
                ffn(store_seq=seq, next_seq=nxt, hook=ffn2_hook,
                    ln_pre=(0, 1), lora_pre=True, t1_pre=(1,), lora2_first_tile=True)
            else:
                if seq > 0:
                    load_R(seq)
                if n_sub >= 1:
                    ffn()
                if n_sub >= 2:
                    attn()
                st_toks.append(store_R(seq))
        p.emit("pool", lambda e: e.engine_nop(), deps=st_toks, tag="final_wait")
        p.emit("act", lambda e: e.mul(srow_c[0:1, 0:1], srow_c[0:1, 0:1], 1.0),
               deps=B("srow_c").deps_write(), tag="trail_nop_act")
        p.emit("dve", lambda e: e.engine_nop(), deps=[], tag="trail_nop_dve")

        stats = p.finalize_and_emit(ctx)
        stats["sbuf_left"] = sbuf_left
    return nc, stats



def shard_inputs(L0_attn_x, L0_attn_gamma, L0_attn_w_qkv, L0_attn_w_out,
                 L0_ffn_W1, L0_ffn_b1, L0_ffn_W2, L0_ffn_b2, L0_ffn_scale):
    f32 = np.float32
    com = {
        "w1T": np.ascontiguousarray(np.asarray(L0_ffn_W1, f32).T),
        "w2T": np.ascontiguousarray(np.asarray(L0_ffn_W2, f32).T),
        "wqkvT": np.ascontiguousarray(np.asarray(L0_attn_w_qkv, f32).T),
        "woutT": np.ascontiguousarray(np.asarray(L0_attn_w_out, f32).T),
        "b1h": np.ascontiguousarray(np.asarray(L0_ffn_b1, f32).reshape(HC, 128).T),
        "b2h": np.ascontiguousarray(np.asarray(L0_ffn_b2, f32).reshape(CC, 128).T),
        "gamh": np.ascontiguousarray(np.asarray(L0_attn_gamma, f32).reshape(CC, 128).T),
        "scaleh": np.asarray(L0_ffn_scale, f32).reshape(1, 1),
    }
    x = np.asarray(L0_attn_x, f32)
    maps = []
    for core in range(x.shape[0] // BPC):
        xs = x[core * BPC:(core + 1) * BPC]
        xTc = np.ascontiguousarray(xs.transpose(0, 2, 1)).reshape(BPC, CC, 128, S)
        maps.append({**com, "xT": xTc})
    return maps


def gather_outputs(results):
    outs = []
    for r in results:
        o = np.asarray(r["outT"]).reshape(BPC, DM, S).transpose(0, 2, 1)
        outs.append(o)
    return np.ascontiguousarray(np.concatenate(outs, axis=0))



from concourse.bass_utils import run_bass_kernel_spmd


def kernel(**inputs):
    maps = shard_inputs(**inputs)
    nc, _stats = build(3)
    res = run_bass_kernel_spmd(nc, maps, core_ids=list(range(8)))
    return gather_outputs(res.results)
